# version 06
# baseline (speedup 1.0000x reference)
import numpy as np
from contextlib import ExitStack

import concourse.bass as bass
import concourse.mybir as mybir
from concourse.bass_utils import run_bass_kernel_spmd
from concourse.bass_core_rust import IndirectOffsetOnAxis

F32 = mybir.dt.float32
BF16 = mybir.dt.bfloat16
AF = mybir.ActivationFunctionType
ALU = mybir.AluOpType
AX = mybir.AxisListType

B, S, D = 4, 4096, 2048
NCORES = 8
TPC = (B * S) // NCORES
H_FF = 1408
NE = 16
NEX = NE + 1
NH, DH, DM = 16, 128, 64
FC = D // 128
HC = H_FF // 128
TG = 1024
NG = TPC // TG
EPS_LN = 1e-5
EPS_AT = 1e-6

ENGS = ("sync", "scalar", "vector", "tensor", "gpsimd")

CAP1 = 384
CAP3 = 512
CAPM = 512


def build_program():
    nc = bass.Bass("TRN2", target_bir_lowering=False, debug=False)

    x0 = nc.dram_tensor("x0", [FC, 128, TPC], F32, kind="ExternalInput").ap()
    w1R = nc.dram_tensor("w1R", [NEX, HC, 128, FC * 128], F32, kind="ExternalInput").ap()
    w2R = nc.dram_tensor("w2R", [NEX, FC, 128, HC * 128], F32, kind="ExternalInput").ap()
    wqT = nc.dram_tensor("wqT", [FC, 128, FC * 128], F32, kind="ExternalInput").ap()
    wkT = nc.dram_tensor("wkT", [FC, 128, FC * 128], F32, kind="ExternalInput").ap()
    wvT = nc.dram_tensor("wvT", [FC, 128, FC * 128], F32, kind="ExternalInput").ap()
    woT = nc.dram_tensor("woT", [FC, 128, FC * 128], F32, kind="ExternalInput").ap()
    gateT = nc.dram_tensor("gateT", [128, FC * NE], F32, kind="ExternalInput").ap()
    biasC = nc.dram_tensor("biasC", [128, 4 * FC], F32, kind="ExternalInput").ap()
    wmapT = nc.dram_tensor("wmapT", [128, 2 * DM], F32, kind="ExternalInput").ap()
    ident_in = nc.dram_tensor("ident_in", [128, 128], F32, kind="ExternalInput").ap()
    expbase_in = nc.dram_tensor("expbase_in", [16, 2], F32, kind="ExternalInput").ap()

    y_out = nc.dram_tensor("y", [FC, 128, TPC], F32, kind="ExternalOutput").ap()
    x_state = nc.dram_tensor("x_state", [FC, 128, TPC], F32, kind="Internal").ap()
    cc_inA = nc.dram_tensor("cc_inA", [128, 528], F32, kind="Internal").ap()
    cc_outA = nc.dram_tensor("cc_outA", [128, 528], F32, kind="Internal").ap()
    cc_inB = nc.dram_tensor("cc_inB", [128, 528], F32, kind="Internal").ap()
    cc_outB = nc.dram_tensor("cc_outB", [128, 528], F32, kind="Internal").ap()
    exp_xn = nc.dram_tensor("exp_xn", [NE * CAPM, D], BF16, kind="Internal").ap()
    y_dram = nc.dram_tensor("y_dram", [NE * CAPM, D], F32, kind="Internal").ap()
    w1E = nc.dram_tensor("w1E", [NEX, HC, 128, FC * 128], BF16, kind="Internal").ap()
    w2E = nc.dram_tensor("w2E", [NEX, FC, 128, HC * 128], BF16, kind="Internal").ap()

    ctx = ExitStack()
    sb = lambda name, shape, dt: ctx.enter_context(nc.sbuf_tensor(name, shape, dt))

    AR1 = sb("ar1", [128, FC * TG], F32)
    AR1b = AR1.bitcast(BF16)
    AR2 = sb("ar2", [128, FC * TG], BF16)
    AR2f = AR2.bitcast(F32)
    HB = sb("hb", [128, 9728], BF16)
    HBf = HB.bitcast(F32)
    SHH = sb("shh", [128, HC * 512], BF16)
    hstage = lambda jc, ts: (SHH[:, jc * 512:jc * 512 + 512] if ts == 0
                             else HB[:, 4096 + jc * 512:4096 + jc * 512 + 512])
    PAN = [sb(f"pan{i}", [128, FC * 128], BF16) for i in range(4)]
    XT = sb("xts", [128, 2 * TPC], F32)
    S1 = sb("s1", [128, 2 * TPC], BF16)
    ROWS = sb("rows", [1, 5 * 512], F32)
    R_MU = ROWS[0:1, 0:512]
    R_VAR = ROWS[0:1, 512:1024]
    R_RR = ROWS[0:1, 1024:1536]
    R_NMR = ROWS[0:1, 1536:2048]
    R_SCR = ROWS[0:1, 2048:2560]
    NROWS = ROWS
    RT = sb("rt", [128, 128], F32)
    CMB = sb("cmb", [16, TG], F32)
    BIAS = sb("biasb", [128, 4 * FC], F32)
    GTE = sb("gte", [128, FC * NE], BF16)
    WMB = sb("wmb", [128, 2 * DM], BF16)
    IDENT = sb("identb", [128, 128], F32)
    ONESb = sb("onesb", [128, 4], BF16)
    ONESf = sb("onesf", [1, 128], F32)
    KVP = sb("kvp", [128, 8 * 132], F32)
    KVB = sb("kvb", [128, 8 * 132], BF16)
    IND0B = sb("ind0b", [128, 256], F32)
    INDAB = sb("indab", [128, 256], F32)
    CWTB = sb("cwtb", [128, 256], F32)
    RTS = sb("rts", [128, 64], F32)
    DSTIB = sb("dstib", [128, 32], mybir.dt.int32)
    CRB = sb("crb", [128, 32], F32)
    CNTB = sb("cntb", [16, 2], F32)
    EXPB = sb("expb", [16, 2], F32)
    IDENTB = sb("identbb", [128, 128], BF16)
    S1f = S1.bitcast(F32)
    XTb = XT.bitcast(BF16)

    PS = [ctx.enter_context(nc.psum_tensor(f"psb{i}", [128, 1024], F32)) for i in range(4)]
    PSb = [p.bitcast(BF16) for p in PS]

    def bank(i):
        return PS[i // 2], (i % 2) * 512

    def bankb(i):
        return PSb[i // 2], (i % 2) * 1024

    sems = {n: ctx.enter_context(nc.semaphore(n)) for n in
            ("dms", "dmg", "dml", "pe", "act", "dve", "ccs", "cc2")}
    cnt = {n: 0 for n in sems}
    prog = {e: [] for e in ENGS}
    OWNER = {"pe": "tensor", "act": "scalar", "dve": "vector"}

    pending = {e: [] for e in ENGS}

    def add_entry(toks):
        for e in ENGS:
            pending[e].extend(t for t in toks if t is not None)

    SELF_SER = {"vector": "dve", "scalar": "act"}

    def op(eng, fn, waits=(), inc=None, amt=1):
        w = [t for t in waits if t is not None]
        if pending[eng]:
            w = pending[eng] + w
            pending[eng] = []
        w = tuple((s, v) for (s, v) in w if v > 0 and OWNER.get(s) != eng)
        if inc is None and eng in SELF_SER:
            inc = SELF_SER[eng]
        tok = None
        if inc is not None:
            cnt[inc] += amt
            tok = (inc, cnt[inc])
        prog[eng].append((fn, w, inc, amt))
        return tok

    def dma(queue, out_ap, in_ap, waits=()):
        q = {"sync": "dms", "gpsimd": "dmg", "scalar": "dml"}[queue]
        return op(queue, lambda e, o=out_ap, i=in_ap: e.dma_start(out=o, in_=i),
                  waits=waits, inc=q, amt=16)

    NOW = lambda s: (s, cnt[s])

    dma("sync", BIAS[:, :], biasC[:, :])
    dma("sync", IDENT[:, :], ident_in[:, :])
    dma("gpsimd", GTE[:, :], gateT[:, :])
    dma("gpsimd", WMB[:, :], wmapT[:, :])
    dma("sync", EXPB[:, :], expbase_in[:, :])
    dma("gpsimd", IDENTB[:, :], ident_in[:, :])
    op("vector", lambda e: e.memset(ONESb[:, :], 1.0), inc="dve")
    op("vector", lambda e: e.memset(ONESf[:, :], 1.0), inc="dve")
    CONST_DMS = NOW("dms")
    CONST_DMG = NOW("dmg")
    CONST_DVE = NOW("dve")

    pan_state = [{"r": None, "e": None} for _ in range(4)]
    pan_ptr = {"i": 0}

    def load_panel(src_ap, echo_to=None):
        i = pan_ptr["i"] % 4
        pan_ptr["i"] += 1
        tok = dma("gpsimd", PAN[i][:, 0:src_ap.shape[-1]], src_ap,
                  waits=[pan_state[i]["r"], pan_state[i]["e"]])
        if echo_to is not None:
            pan_state[i]["e"] = dma("scalar", echo_to,
                                    PAN[i][:, 0:src_ap.shape[-1]], waits=[tok])
        return i, tok

    def mark_panel(i, pe_tok):
        pan_state[i]["r"] = pe_tok

    def ln_seg_chain(bsum_i, bsq_i, pe_stats, war_pe):
        bsum, csum = bank(bsum_i)
        bsq_, csq = bank(bsq_i)
        op("vector", lambda e: e.tensor_scalar(out=R_MU, in0=bsum[0:1, csum:csum + 512],
                                               scalar1=1.0 / D, scalar2=None, op0=ALU.mult),
           waits=[pe_stats, war_pe])
        op("vector", lambda e: e.tensor_scalar(out=R_VAR, in0=bsq_[0:1, csq:csq + 512],
                                               scalar1=1.0 / D, scalar2=EPS_LN,
                                               op0=ALU.mult, op1=ALU.add))
        op("vector", lambda e: e.tensor_tensor(out=R_SCR, in0=R_MU, in1=R_MU, op=ALU.mult))
        tvar = op("vector", lambda e: e.tensor_tensor(out=R_VAR, in0=R_VAR, in1=R_SCR,
                                                      op=ALU.subtract), inc="dve")
        tstd = op("scalar", lambda e: e.activation(out=R_VAR, in_=R_VAR, func=AF.Sqrt),
                  waits=[tvar], inc="act")
        trec = op("vector", lambda e: e.reciprocal(out=R_RR, in_=R_VAR), waits=[tstd], inc="dve")
        tnmr = op("vector", lambda e: e.scalar_tensor_tensor(
            out=R_NMR, in0=R_MU, scalar=-1.0, in1=R_RR, op0=ALU.mult, op1=ALU.mult), inc="dve")
        op("tensor", lambda e: e.matmul(bsum[:, csum:csum + 512], ONESf[0:1, :], R_RR,
                                        start=True, stop=True), waits=[trec, CONST_DVE])
        tok = op("tensor", lambda e: e.matmul(bsq_[:, csq:csq + 512], ONESf[0:1, :], R_NMR,
                                              start=True, stop=True), waits=[tnmr], inc="pe")
        return tok

    pairs = [(0, 1), (0, 2), (0, 3), (1, 2), (1, 3), (2, 3)]

    def emit_ffn_sparse(src, dst, cap, expcol, echo_write):
        IOA = IndirectOffsetOnAxis
        ci = cap // 128

        fs = {
            "pe_stats_par": {0: None, 1: None},
            "stats_war": NOW("dve"),
            "scat_toks": [],
            "bank_rd": {i: None for i in range(8)},
            "t_cnt": None,
            "prev_tev": None,
        }
        bank_rd = fs["bank_rd"]

        def emit_front_ln(g):
            t0 = g * TG
            ld = []
            for fc in range(FC):
                ld.append(dma("sync", AR1[:, fc * TG:(fc + 1) * TG],
                              src[fc, :, t0:t0 + TG]))
            for fc in range(FC):
                xg = AR1[:, fc * TG:(fc + 1) * TG]
                xb = S1[:, (fc % 2) * TG:(fc % 2) * TG + TG]
                sq = S1[:, 2 * TG + (fc % 2) * TG:2 * TG + (fc % 2) * TG + TG]
                tcast = op("vector", lambda e, o=xb, i=xg: e.tensor_copy(out=o, in_=i),
                           waits=[ld[fc], fs["pe_stats_par"][fc % 2]])
                tsq = op("scalar", lambda e, o=sq, i=xb: e.activation(out=o, in_=i, func=AF.Square),
                         waits=[tcast])
                first, last = fc == 0, fc == FC - 1
                for seg in range(2):
                    bsum, csum = bank(4 + 2 * seg)
                    bsq_, csq = bank(5 + 2 * seg)
                    wmm = [tcast, tsq] + ([fs["stats_war"]] if first else [])
                    op("tensor", lambda e, o=bsum[0:1, csum:csum + 512], l=ONESb[:, 0:1],
                       r=xb[:, seg * 512:seg * 512 + 512], st=first, sp=last:
                       e.matmul(o, l, r, start=st, stop=sp), waits=wmm if first else [tcast, tsq])
                    ptok = op("tensor", lambda e, o=bsq_[0:1, csq:csq + 512], l=ONESb[:, 1:2],
                              r=sq[:, seg * 512:seg * 512 + 512], st=first, sp=last:
                              e.matmul(o, l, r, start=st, stop=sp), inc="pe")
                fs["pe_stats_par"][fc % 2] = ptok
            pe_stats = NOW("pe")
            pe_bc = None
            for seg in range(2):
                pe_bc = ln_seg_chain(4 + 2 * seg, 5 + 2 * seg, pe_stats, pe_bc)
            for fc in range(FC):
                for seg in range(2):
                    br, cr = bank(4 + 2 * seg)
                    bn, cn = bank(5 + 2 * seg)
                    xgs = AR1[:, fc * TG + seg * 512:fc * TG + seg * 512 + 512]
                    tts = XT[:, (fc % 4) * 512:(fc % 4) * 512 + 512]
                    xns = AR2[:, fc * TG + seg * 512:fc * TG + seg * 512 + 512]
                    op("vector", lambda e, o=tts, a=xgs, b=br[:, cr:cr + 512]:
                       e.tensor_tensor(out=o, in0=a, in1=b, op=ALU.mult), waits=[pe_bc])
                    op("vector", lambda e, o=xns, a=tts, b=bn[:, cn:cn + 512]:
                       e.tensor_tensor(out=o, in0=a, in1=b, op=ALU.add))
            return NOW("dve"), pe_stats

        def routing_scatter_gen(g, dve_xn, pe_stats):
            prev_tcw = None
            prev_ttr = None
            rt_sig = None
            rt_toks = []
            for tt in range(TG // 128):
                gi = g * 8 + tt
                bs, cs = bank(6)
                for kc in range(FC):
                    st, sp = kc == 0, kc == FC - 1
                    op("tensor", lambda e, o=bs[0:128, cs:cs + 16],
                       l=AR2[:, kc * TG + tt * 128:kc * TG + tt * 128 + 128],
                       r=GTE[:, kc * NE:(kc + 1) * NE], st=st, sp=sp:
                       e.matmul(o, l, r, start=st, stop=sp),
                       waits=[dve_xn, CONST_DMG, rt_sig] if st else (),
                       inc="pe" if sp else None)
                pe_sc = ("pe", cnt["pe"])
                yield
                rt_sig = op("scalar", lambda e, o=RT[:, 0:16], i=bs[0:128, cs:cs + 16]:
                            e.activation(out=o, in_=i, func=AF.Sigmoid),
                            waits=[pe_sc, prev_tcw])
                for pi, (a, b) in enumerate(pairs):
                    op("vector", lambda e, a_=a, b_=b, pi_=pi: e.tensor_tensor(
                        out=RT[:, 16 + 4 * pi_:20 + 4 * pi_], in0=RT[:, a_:13 + a_:4],
                        in1=RT[:, b_:13 + b_:4], op=ALU.add),
                        waits=[rt_sig] if pi == 0 else ())
                op("vector", lambda e: e.tensor_tensor(out=RT[:, 40:44], in0=RT[:, 16:20],
                                                       in1=RT[:, 20:24], op=ALU.max))
                for pi in range(2, 6):
                    op("vector", lambda e, pi_=pi: e.tensor_tensor(
                        out=RT[:, 40:44], in0=RT[:, 40:44],
                        in1=RT[:, 16 + 4 * pi_:20 + 4 * pi_], op=ALU.max))
                for pi, (a, b) in enumerate(pairs):
                    op("vector", lambda e, a_=a, b_=b, pi_=pi: e.tensor_tensor(
                        out=RT[:, 44 + pi_:45 + pi_], in0=RT[:, 40 + a_:41 + a_],
                        in1=RT[:, 40 + b_:41 + b_], op=ALU.min))
                op("vector", lambda e: e.tensor_tensor(out=RT[:, 50:51], in0=RT[:, 44:45],
                                                       in1=RT[:, 45:46], op=ALU.max))
                for pi in range(2, 6):
                    op("vector", lambda e, pi_=pi: e.tensor_tensor(
                        out=RT[:, 50:51], in0=RT[:, 50:51],
                        in1=RT[:, 44 + pi_:45 + pi_], op=ALU.max))
                op("vector", lambda e: e.tensor_scalar(out=RT[:, 52:56], in0=RT[:, 40:44],
                                                       scalar1=RT[:, 50:51], scalar2=None,
                                                       op0=ALU.is_ge))
                for gidx in range(4):
                    op("vector", lambda e, g_=gidx: e.tensor_scalar(
                        out=RT[:, 56 + 4 * g_:60 + 4 * g_], in0=RT[:, 4 * g_:4 * g_ + 4],
                        scalar1=RT[:, 52 + g_:53 + g_], scalar2=None, op0=ALU.mult),
                        waits=[prev_ttr] if gidx == 0 else ())
                op("vector", lambda e: e.reduce_max(out=RT[:, 72:73], in_=RT[:, 56:72], axis=AX.X))
                op("vector", lambda e: e.tensor_scalar(out=RT[:, 73:89], in0=RT[:, 56:72],
                                                       scalar1=RT[:, 72:73], scalar2=None,
                                                       op0=ALU.is_ge))
                op("vector", lambda e: e.scalar_tensor_tensor(
                    out=RT[:, 89:105], in0=RT[:, 73:89], scalar=-1e30, in1=RT[:, 56:72],
                    op0=ALU.mult, op1=ALU.add))
                op("vector", lambda e: e.reduce_max(out=RT[:, 105:106], in_=RT[:, 89:105], axis=AX.X))
                op("vector", lambda e: e.tensor_scalar(out=RT[:, 106:122], in0=RT[:, 56:72],
                                                       scalar1=RT[:, 105:106], scalar2=None,
                                                       op0=ALU.is_ge))
                op("vector", lambda e: e.tensor_scalar(out=RT[:, 122:123], in0=RT[:, 72:73],
                                                       scalar1=RT[:, 105:106], scalar2=1e-20,
                                                       op0=ALU.add, op1=ALU.add))
                op("vector", lambda e: e.reciprocal(out=RT[:, 123:124], in_=RT[:, 122:123]))
                op("vector", lambda e: e.tensor_tensor(out=RT[:, 89:105], in0=RT[:, 56:72],
                                                       in1=RT[:, 106:122], op=ALU.mult))
                tcw = op("vector", lambda e: e.tensor_scalar(
                    out=RT[:, 56:72], in0=RT[:, 89:105], scalar1=RT[:, 123:124], scalar2=None,
                    op0=ALU.mult))
                prev_tcw = tcw
                op("vector", lambda e, c=gi * 16: e.tensor_copy(out=CWTB[:, c:c + 16],
                                                                in_=RT[:, 56:72]))
                op("vector", lambda e, c=gi * 16: e.tensor_copy(out=IND0B[:, c:c + 16],
                                                                in_=RT[:, 73:89]))
                op("vector", lambda e, c=gi * 16: e.tensor_copy(out=INDAB[:, c:c + 16],
                                                                in_=RT[:, 106:122]))
                bt, ct = bank(7)
                ttr = op("tensor", lambda e, o=bt[0:16, ct:ct + 128]:
                         e.transpose(o, RT[:, 56:72], IDENT[:, :]),
                         waits=[NOW("dve"), CONST_DMS, fs["prev_tev"]], inc="pe")
                prev_ttr = ttr
                tev = op("vector", lambda e, o=CMB[0:16, tt * 128:tt * 128 + 128],
                         i=bt[0:16, ct:ct + 128]: e.tensor_copy(out=o, in_=i),
                         waits=[ttr])
                fs["prev_tev"] = tev
                rt_toks.append(tev)
                yield
            selv = XT[0:16, 2048:3072]
            dstf = XT[0:16, 3072:4096]
            cur = S1f[0:16, 0:1024]
            oth = S1f[0:16, 1024:2048]
            op("vector", lambda e: e.tensor_scalar(out=selv, in0=CMB[0:16, :], scalar1=0.0,
                                                   scalar2=None, op0=ALU.is_gt),
               waits=rt_toks + [pe_stats])
            op("vector", lambda e: e.tensor_copy(out=cur, in_=selv))
            sh = 1
            while sh < 1024:
                op("vector", lambda e, a=oth[0:16, 0:sh], b=cur[0:16, 0:sh]:
                   e.tensor_copy(out=a, in_=b))
                op("vector", lambda e, o=oth[0:16, sh:1024], a=cur[0:16, sh:1024],
                   b=cur[0:16, 0:1024 - sh]:
                   e.tensor_tensor(out=o, in0=a, in1=b, op=ALU.add))
                cur, oth = oth, cur
                sh *= 2
            op("vector", lambda e, c=cur: e.tensor_tensor(out=dstf, in0=c, in1=selv,
                                                          op=ALU.subtract))
            if g == 0:
                fs["t_cnt"] = op("vector", lambda e, c=cur: e.tensor_scalar(
                    out=CNTB[0:16, 0:1], in0=c[0:16, 1023:1024],
                    scalar1=float(cap - 1), scalar2=None, op0=ALU.min))
            else:
                op("vector", lambda e: e.tensor_scalar(out=dstf, in0=dstf,
                                                       scalar1=CNTB[0:16, 0:1], scalar2=None,
                                                       op0=ALU.add), waits=[fs["t_cnt"]])
            op("vector", lambda e: e.tensor_scalar(out=dstf, in0=dstf,
                                                   scalar1=float(cap - 1), scalar2=None,
                                                   op0=ALU.min))
            t_dstf = op("vector", lambda e: e.tensor_scalar(
                out=dstf, in0=dstf, scalar1=EXPB[0:16, expcol:expcol + 1], scalar2=None,
                op0=ALU.add), waits=[CONST_DMS])
            yield
            for tt in range(TG // 128):
                gi = g * 8 + tt
                bt, ct = bank(7)
                ttr2 = op("tensor", lambda e, o=bt[0:128, ct:ct + 16],
                          i=dstf[0:16, tt * 128:tt * 128 + 128]:
                          e.transpose(o, i, IDENT[0:16, 0:16]),
                          waits=[t_dstf, bank_rd[7], fs["prev_tev"]], inc="pe")
                tdstt = op("vector", lambda e, i=bt[0:128, ct:ct + 16]:
                           e.tensor_copy(out=RTS[:, 0:16], in_=i), waits=[ttr2])
                bank_rd[7] = tdstt
                op("vector", lambda e, c=gi * 16: e.tensor_tensor(
                    out=RTS[:, 16:32], in0=INDAB[:, c:c + 16], in1=IND0B[:, c:c + 16],
                    op=ALU.subtract))
                for r in range(2):
                    c16 = gi * 16
                    ind_ap = IND0B[:, c16:c16 + 16] if r == 0 else RTS[:, 16:32]
                    op("vector", lambda e, ia=ind_ap: e.tensor_tensor(
                        out=RTS[:, 32:48], in0=RTS[:, 0:16], in1=ia, op=ALU.mult))
                    op("vector", lambda e: e.reduce_sum(out=RTS[:, 48:49], in_=RTS[:, 32:48],
                                                        axis=AX.X))
                    op("vector", lambda e, c=gi * 2 + r: e.tensor_copy(
                        out=DSTIB[:, c:c + 1], in_=RTS[:, 48:49]))
                    op("vector", lambda e, ia=ind_ap, c=c16: e.tensor_tensor(
                        out=RTS[:, 32:48], in0=CWTB[:, c:c + 16], in1=ia, op=ALU.mult))
                    op("vector", lambda e, c=gi * 2 + r: e.reduce_sum(
                        out=CRB[:, c:c + 1], in_=RTS[:, 32:48], axis=AX.X))
                t_rank = NOW("dve")
                xbuf = HB[:, (tt % 2) * 2048:(tt % 2) * 2048 + 2048]
                ev_toks = []
                for kc in range(FC):
                    bslot = 4 + kc % 2
                    bxb, cxb = bankb(bslot)
                    w = [dve_xn, CONST_DMG, bank_rd[bslot]]
                    ttx = op("tensor", lambda e, o=bxb[:, cxb:cxb + 128],
                             i=AR2[:, kc * TG + tt * 128:kc * TG + tt * 128 + 128]:
                             e.transpose(o, i, IDENTB[:, :]), waits=w, inc="pe")
                    wv2 = [ttx]
                    if kc == 0 and len(fs["scat_toks"]) >= 2:
                        wv2.append(fs["scat_toks"][-2])
                    tev2 = op("vector", lambda e, o=xbuf[:, kc * 128:(kc + 1) * 128],
                              i=bxb[:, cxb:cxb + 128]: e.tensor_copy(out=o, in_=i),
                              waits=wv2)
                    bank_rd[bslot] = tev2
                    ev_toks.append(tev2)
                for r in range(2):
                    fs["scat_toks"].append(op("gpsimd", lambda e, c=gi * 2 + r, xb_=xbuf:
                                              e.indirect_dma_start(
                                                  exp_xn[:, :], IOA(DSTIB[:, c:c + 1], 0),
                                                  xb_[:, :], None),
                                              waits=[t_rank, ev_toks[-1]], inc="dmg", amt=16))
                yield

        def shared_gen(g, dve_xn, sh_toks):
            if echo_write and g == 0:
                ld_w1 = lambda jc: load_panel(w1R[0, jc, :, :], echo_to=w1E[0, jc, :, :])
                ld_w2 = lambda dc: load_panel(w2R[0, dc, :, :], echo_to=w2E[0, dc, :, :])
                bar = None
            else:
                pending["gpsimd"].append(("dml", cnt["dml"]))
                ld_w1 = lambda jc: load_panel(w1E[0, jc, :, :])
                ld_w2 = lambda dc: load_panel(w2E[0, dc, :, :])
            for jc in range(HC):
                p_i, p_tok = ld_w1(jc)
                for ts in range(2):
                    bsl = (2 * jc + ts) % 2
                    bh, ch = bank(bsl)
                    wvm = [p_tok, dve_xn, bank_rd[bsl]]
                    for kc in range(FC):
                        st, sp = kc == 0, kc == FC - 1
                        op("tensor", lambda e, o=bh[:, ch:ch + 512],
                           l=PAN[p_i][:, kc * 128:(kc + 1) * 128],
                           r=AR2[:, kc * TG + ts * 512:kc * TG + ts * 512 + 512],
                           st=st, sp=sp: e.matmul(o, l, r, start=st, stop=sp),
                           waits=wvm if st else (), inc="pe" if sp else None)
                    pe_mm = ("pe", cnt["pe"])
                    mark_panel(p_i, pe_mm)
                    tact = op("scalar", lambda e, o=hstage(jc, ts),
                              i=bh[:, ch:ch + 512]: e.activation(out=o, in_=i, func=AF.Silu),
                              waits=[pe_mm])
                    bank_rd[bsl] = tact
                yield
            silu_last = tact
            for dc in range(FC):
                p_i, p_tok = ld_w2(dc)
                for ts in range(2):
                    bsl = 2 + (2 * dc + ts) % 2
                    by, cy = bank(bsl)
                    wvm = [p_tok, silu_last, bank_rd[bsl]]
                    for hc in range(HC):
                        st, sp = hc == 0, hc == HC - 1
                        op("tensor", lambda e, o=by[:, cy:cy + 512],
                           l=PAN[p_i][:, hc * 128:(hc + 1) * 128],
                           r=hstage(hc, ts), st=st, sp=sp:
                           e.matmul(o, l, r, start=st, stop=sp),
                           waits=wvm if st else (), inc="pe" if sp else None)
                    pe_mm = ("pe", cnt["pe"])
                    mark_panel(p_i, pe_mm)
                    xgs = AR1[:, dc * TG + ts * 512:dc * TG + ts * 512 + 512]
                    tacc = op("vector", lambda e, o=xgs, a=by[:, cy:cy + 512]:
                              e.scalar_tensor_tensor(out=o, in0=a, scalar=0.5, in1=o,
                                                     op0=ALU.mult, op1=ALU.add),
                              waits=[pe_mm, sh_toks[dc]])
                    bank_rd[bsl] = tacc
                    sh_toks[dc] = tacc
                yield

        for g in range(NG):
            t0 = g * TG
            dve_xn, pe_stats = emit_front_ln(g)
            sh_toks = {fc: None for fc in range(FC)}
            rg = routing_scatter_gen(g, dve_xn, pe_stats)
            sg = shared_gen(g, dve_xn, sh_toks)
            r_alive = s_alive = True
            while r_alive or s_alive:
                if r_alive:
                    try:
                        next(rg)
                    except StopIteration:
                        r_alive = False
                if s_alive:
                    try:
                        next(sg)
                    except StopIteration:
                        s_alive = False
            for fc in range(FC):
                dma("sync", dst[fc, :, t0:t0 + TG], AR1[:, fc * TG:(fc + 1) * TG],
                    waits=[sh_toks[fc]])
            add_entry([NOW("dms")])

        intr_pe = {}
        m1_done = None
        ystore_done = {}
        for ei in range(NE):
            eb = ei * cap
            XET = XTb[:, 0:ci * 2048] if ei % 2 == 0 else HB[:, 0:ci * 2048]
            wv = [fs["scat_toks"][-1]] if ei < 2 else [intr_pe[ei - 2]]
            if ei < 2:
                wv.append(NOW("dve"))
            t_ld = dma("gpsimd", XET.rearrange("p (i c) -> p i c", c=D),
                       exp_xn[eb:eb + cap, :].rearrange("(i p) c -> p i c", p=128),
                       waits=wv)
            first_evac = True
            for i in range(ci):
                for kc in range(FC):
                    bslot = 4 + kc % 2
                    bxb, cxb = bankb(bslot)
                    w = [t_ld, bank_rd[bslot]]
                    ttx = op("tensor", lambda e, o=bxb[:, cxb:cxb + 128],
                             i=XET[:, i * 2048 + kc * 128:i * 2048 + kc * 128 + 128]:
                             e.transpose(o, i, IDENTB[:, :]), waits=w, inc="pe")
                    wev = [ttx]
                    if first_evac:
                        wev.append(m1_done)
                        if ei == 0:
                            wev.append(NOW("dms"))
                        first_evac = False
                    tev2 = op("vector", lambda e, o=AR1b[:, kc * cap + i * 128:
                                                         kc * cap + i * 128 + 128],
                              i=bxb[:, cxb:cxb + 128]: e.tensor_copy(out=o, in_=i),
                              waits=wev)
                    bank_rd[bslot] = tev2
            intr_pe[ei] = ("pe", cnt["pe"])
            t_intr = NOW("dve")
            silu_last = None
            for jc in range(HC):
                if echo_write:
                    p_i, p_tok = load_panel(w1R[ei + 1, jc, :, :],
                                            echo_to=w1E[ei + 1, jc, :, :])
                else:
                    p_i, p_tok = load_panel(w1E[ei + 1, jc, :, :])
                bsl = jc % 2
                bh, ch = bank(bsl)
                wvm = [p_tok, t_intr, bank_rd[bsl]]
                for kc in range(FC):
                    st, sp = kc == 0, kc == FC - 1
                    op("tensor", lambda e, o=bh[:, ch:ch + cap],
                       l=PAN[p_i][:, kc * 128:(kc + 1) * 128],
                       r=AR1b[:, kc * cap:kc * cap + cap], st=st, sp=sp:
                       e.matmul(o, l, r, start=st, stop=sp),
                       waits=wvm if st else (), inc="pe" if sp else None)
                pe_mm = ("pe", cnt["pe"])
                mark_panel(p_i, pe_mm)
                tact = op("scalar", lambda e, o=AR1b[:, 8192 + jc * cap:8192 + jc * cap + cap],
                          i=bh[:, ch:ch + cap]: e.activation(out=o, in_=i, func=AF.Silu),
                          waits=[pe_mm])
                bank_rd[bsl] = tact
                silu_last = tact
            m1_done = ("pe", cnt["pe"])
            for dc in range(FC):
                if echo_write:
                    p_i, p_tok = load_panel(w2R[ei + 1, dc, :, :],
                                            echo_to=w2E[ei + 1, dc, :, :])
                else:
                    p_i, p_tok = load_panel(w2E[ei + 1, dc, :, :])
                bsl = 2 + dc % 2
                by, cy = bank(bsl)
                wvm = [p_tok, silu_last, bank_rd[bsl]]
                for hc in range(HC):
                    st, sp = hc == 0, hc == HC - 1
                    op("tensor", lambda e, o=by[:, cy:cy + cap],
                       l=PAN[p_i][:, hc * 128:(hc + 1) * 128],
                       r=AR1b[:, 8192 + hc * cap:8192 + hc * cap + cap], st=st, sp=sp:
                       e.matmul(o, l, r, start=st, stop=sp),
                       waits=wvm if st else (), inc="pe" if sp else None)
                pe_mm = ("pe", cnt["pe"])
                mark_panel(p_i, pe_mm)
                yfc = S1f[:, (dc % 2) * cap:(dc % 2) * cap + cap]
                wve = [pe_mm]
                if ei == 0 and dc < 2:
                    wve.append(NOW("dve"))
                tev3 = op("vector", lambda e, o=yfc, i=by[:, cy:cy + cap]:
                          e.tensor_copy(out=o, in_=i), waits=wve)
                bank_rd[bsl] = tev3
                ytc0 = 2 * cap + (dc % 2) * cap
                ytoks = []
                for i in range(ci):
                    bslot = 6 + i % 2
                    byt, cyt = bank(bslot)
                    w = [tev3, bank_rd[bslot]]
                    tty = op("tensor", lambda e, o=byt[:, cyt:cyt + 128],
                             i=yfc[:, i * 128:i * 128 + 128]:
                             e.transpose(o, i, IDENT[:, :]), waits=w, inc="pe")
                    wev4 = [tty]
                    if i == 0:
                        wev4.append(ystore_done.get(dc % 2))
                    tev4 = op("vector", lambda e, o=S1f[:, ytc0 + i * 128:ytc0 + i * 128 + 128],
                              i=byt[:, cyt:cyt + 128]: e.tensor_copy(out=o, in_=i),
                              waits=wev4)
                    bank_rd[bslot] = tev4
                    ytoks.append(tev4)
                ystore_done[dc % 2] = dma(
                    "sync", y_dram[eb:eb + cap, dc * 128:(dc + 1) * 128].rearrange(
                        "(i p) c -> p i c", p=128),
                    S1f[:, ytc0:ytc0 + cap].rearrange("p (i c) -> p i c", c=128),
                    waits=ytoks)
        PE_EXP = NOW("pe")
        Y_DMS = NOW("dms")
        DVE_EXP = NOW("dve")
        gbA = lambda q: XT[:, q * 2048:(q + 1) * 2048]
        gbB = lambda q: HBf[:, q * 2048:(q + 1) * 2048]
        cbv = lambda q: AR2f[:, q * 2048:(q + 1) * 2048]
        g_tok = {}
        cb_done = {0: None, 1: None}
        tr_done = {0: PE_EXP, 1: PE_EXP}

        def issue_gather(fi, extra=()):
            q = fi % 2
            w0 = list(extra) + [cb_done[q]]
            t0g = op("gpsimd", lambda e, c=2 * fi, q_=q: e.indirect_dma_start(
                gbA(q_), None, y_dram[:, :], IOA(DSTIB[:, c:c + 1], 0)),
                waits=w0, inc="dmg", amt=16)
            t1g = op("gpsimd", lambda e, c=2 * fi + 1, q_=q: e.indirect_dma_start(
                gbB(q_), None, y_dram[:, :], IOA(DSTIB[:, c:c + 1], 0)),
                inc="dmg", amt=16)
            g_tok[fi] = (t0g, t1g)

        issue_gather(0, extra=[Y_DMS, PE_EXP, DVE_EXP])
        issue_gather(1)
        ld2 = {}
        st_prev = {}
        for g in range(NG):
            t0c = g * TG
            acc_toks = {fc: None for fc in range(FC)}
            for fc in range(FC):
                w = [st_prev.get(fc)]
                if g == 0 and fc == 0:
                    w += [PE_EXP, DVE_EXP, Y_DMS]
                ld2[fc] = dma("sync", AR1[:, fc * TG:(fc + 1) * TG],
                              dst[fc, :, t0c:t0c + TG], waits=w)
            LD_ALL = NOW("dms")
            for tt in range(8):
                fi = g * 8 + tt
                q = fi % 2
                tA, tB = g_tok[fi]
                op("vector", lambda e, q_=q, c=2 * fi: e.tensor_scalar(
                    out=cbv(q_), in0=gbA(q_), scalar1=CRB[:, c:c + 1], scalar2=None,
                    op0=ALU.mult), waits=[tA, tr_done[q]] + ([LD_ALL] if tt == 0 else []))
                c1 = op("vector", lambda e, q_=q, c=2 * fi + 1: e.scalar_tensor_tensor(
                    out=cbv(q_), in0=gbB(q_), scalar=CRB[:, c:c + 1], in1=cbv(q_),
                    op0=ALU.mult, op1=ALU.add), waits=[tB])
                cb_done[q] = c1
                if fi + 2 < NG * 8:
                    issue_gather(fi + 2)
                for fc in range(FC):
                    bslot = 6 + fc % 2
                    bat, cat = bank(bslot)
                    tta = op("tensor", lambda e, o=bat[:, cat:cat + 128], q_=q, f_=fc:
                             e.transpose(o, cbv(q_)[:, f_ * 128:f_ * 128 + 128], IDENT[:, :]),
                             waits=[c1, bank_rd[bslot]], inc="pe")
                    tacc = op("vector", lambda e, o=AR1[:, fc * TG + tt * 128:
                                                       fc * TG + tt * 128 + 128],
                              i=bat[:, cat:cat + 128]:
                              e.scalar_tensor_tensor(out=o, in0=i, scalar=0.5, in1=o,
                                                     op0=ALU.mult, op1=ALU.add),
                              waits=[tta, ld2[fc], acc_toks[fc]])
                    bank_rd[bslot] = tacc
                    acc_toks[fc] = tacc
                tr_done[q] = ("pe", cnt["pe"])
            st_tok = NOW("dve")
            for fc in range(FC):
                st_prev[fc] = dma("sync", dst[fc, :, t0c:t0c + TG],
                                  AR1[:, fc * TG:(fc + 1) * TG], waits=[st_tok])
            add_entry([NOW("dms"), NOW("pe")])
        return

    emit_ffn_sparse(x0, x_state, CAP1, 0, echo_write=True)
    add_entry([NOW("dms")])

    asrc = x_state
    stok = {0: None, 1: None}
    cast_prev = {0: None, 1: None}
    for fc in range(FC):
        par = fc % 2
        xtp = XT[:, par * TPC:par * TPC + TPC]
        ldt = dma("sync", xtp, asrc[fc, :, 0:TPC], waits=[cast_prev[par]])
        xb = S1[:, 0:TPC] if par == 0 else SHH[:, 0:TPC]
        sq = S1[:, TPC:2 * TPC] if par == 0 else SHH[:, TPC:2 * TPC]
        tcast = op("vector", lambda e, o=xb, i=xtp: e.tensor_copy(out=o, in_=i),
                   waits=[ldt, stok[par]], inc="dve")
        cast_prev[par] = tcast
        tsq = op("scalar", lambda e, o=sq, i=xb: e.activation(out=o, in_=i, func=AF.Square),
                 waits=[tcast], inc="act")
        first, last = fc == 0, fc == FC - 1
        for seg in range(4):
            bsum, csum = bank(seg)
            bsq_, csq = bank(4 + seg)
            op("tensor", lambda e, o=bsum[0:1, csum:csum + 512], l=ONESb[:, 0:1],
               r=xb[:, seg * 512:seg * 512 + 512], st=first, sp=last:
               e.matmul(o, l, r, start=st, stop=sp), waits=[tcast, tsq])
            ptok = op("tensor", lambda e, o=bsq_[0:1, csq:csq + 512], l=ONESb[:, 1:2],
                      r=sq[:, seg * 512:seg * 512 + 512], st=first, sp=last:
                      e.matmul(o, l, r, start=st, stop=sp), inc="pe")
        stok[par] = ptok
    pe_st = NOW("pe")
    pe_bc = None
    for seg in range(4):
        pe_bc = ln_seg_chain(seg, 4 + seg, pe_st, pe_bc)
    xn_prev = None
    for fc in range(FC):
        xtp = XT[:, (fc % 2) * TPC:(fc % 2) * TPC + TPC]
        ld2 = dma("sync", xtp, asrc[fc, :, 0:TPC], waits=[xn_prev])
        for seg in range(4):
            br, cr = bank(seg)
            bn, cn = bank(4 + seg)
            xts = xtp[:, seg * 512:seg * 512 + 512]
            scrf = HBf[:, seg * 512:seg * 512 + 512]
            op("vector", lambda e, o=scrf, a=xts, b=br[:, cr:cr + 512]:
               e.tensor_tensor(out=o, in0=a, in1=b, op=ALU.mult), waits=[ld2, pe_bc])
            t2 = op("vector", lambda e,
                    o=AR1b[:, fc * TPC + seg * 512:fc * TPC + seg * 512 + 512],
                    a=scrf, b=bn[:, cn:cn + 512]:
                    e.tensor_tensor(out=o, in0=a, in1=b, op=ALU.add), inc="dve")
        xn_prev = t2
    dve_xn2 = NOW("dve")
    XN = lambda kc: AR1b[:, kc * TPC:(kc + 1) * TPC]

    QB = HB[:, 0:2048]
    KB = HB[:, 2048:4096]
    KT = HB[:, 4096:5120]
    VB = HB[:, 5120:7168]
    VTBASE = 7168
    VT = lambda tt: HB[:, VTBASE + tt * 136:VTBASE + tt * 136 + 136]
    tones_v = op("vector", lambda e: e.memset(
        HB[:, VTBASE + 128:VTBASE + 16 * 136:136], 1.0), waits=[dve_xn2], inc="dve")

    kv_toks = []
    qs_done = {}
    bk_rd = {i: None for i in range(8)}
    t_kvbA = t_kvbB = None
    for h in range(NH):
        base = 64 * (h % 2)
        qcol = (h // 2) * 2048
        pq, tq = load_panel(wqT[h, :, :])
        pk, tk = load_panel(wkT[h, :, :])
        pv, tv_ = load_panel(wvT[h, :, :])
        qb_toks = []
        for ts in range(4):
            bsl = ts % 2
            bq_, cq_ = bank(bsl)
            wv = [tq, dve_xn2, bk_rd[bsl]]
            for kc in range(FC):
                st, sp = kc == 0, kc == FC - 1
                op("tensor", lambda e, o=bq_[:, cq_:cq_ + 512],
                   l=PAN[pq][:, kc * 128:(kc + 1) * 128],
                   r=XN(kc)[:, ts * 512:ts * 512 + 512], st=st, sp=sp:
                   e.matmul(o, l, r, start=st, stop=sp),
                   waits=wv if st else (), inc="pe" if sp else None)
            pe_mm = ("pe", cnt["pe"])
            mark_panel(pq, pe_mm)
            tev = op("vector", lambda e, o=QB[:, ts * 512:ts * 512 + 512],
                     i=bq_[:, cq_:cq_ + 512], b=BIAS[:, 0 * FC + h:0 * FC + h + 1]:
                     e.tensor_scalar(out=o, in0=i, scalar1=b, scalar2=None, op0=ALU.add),
                     waits=[pe_mm, CONST_DMS], inc="dve")
            bk_rd[bsl] = tev
            qb_toks.append(tev)
        qs_t = []
        for ts in range(4):
            bsl = 4 + ts % 2
            bqs, cqs = bank(bsl)
            tmm = op("tensor", lambda e, o=bqs[0:64, cqs:cqs + 512], l=WMB[:, 0:64],
                     r=QB[:, ts * 512:ts * 512 + 512]:
                     e.matmul(o, l, r, start=True, stop=True),
                     waits=[qb_toks[ts], CONST_DMG, bk_rd[bsl]], inc="pe")
            scr1 = XT[0:64, (ts % 2) * 512:(ts % 2) * 512 + 512]
            scr2 = XT[0:64, 1024 + (ts % 2) * 512:1024 + (ts % 2) * 512 + 512]
            t1 = op("vector", lambda e, o=scr1, i=bqs[0:64, cqs:cqs + 512]:
                    e.tensor_scalar_min(o, i, 0.0),
                    waits=[tmm], inc="dve")
            t2 = op("scalar", lambda e, o=scr2, i=scr1:
                    e.activation(out=o, in_=i, func=AF.Exp), waits=[t1], inc="act")
            t3 = op("vector", lambda e,
                    o=AR2[base:base + 64, qcol + ts * 512:qcol + ts * 512 + 512],
                    i=bqs[0:64, cqs:cqs + 512], x2=scr2:
                    e.scalar_tensor_tensor(out=o, in0=i, scalar=0.0, in1=x2,
                                           op0=ALU.max, op1=ALU.add),
                    waits=[t2], inc="dve")
            bk_rd[bsl] = t3
            qs_t.append(t3)
        qs_done[h] = qs_t[-1]
        kb_toks = []
        for ts in range(4):
            bsl = ts % 2
            bk_, ck_ = bank(bsl)
            wv = [tk, dve_xn2, bk_rd[bsl]]
            for kc in range(FC):
                st, sp = kc == 0, kc == FC - 1
                op("tensor", lambda e, o=bk_[:, ck_:ck_ + 512],
                   l=PAN[pk][:, kc * 128:(kc + 1) * 128],
                   r=XN(kc)[:, ts * 512:ts * 512 + 512], st=st, sp=sp:
                   e.matmul(o, l, r, start=st, stop=sp),
                   waits=wv if st else (), inc="pe" if sp else None)
            pe_mm = ("pe", cnt["pe"])
            mark_panel(pk, pe_mm)
            tev = op("vector", lambda e, o=KB[:, ts * 512:ts * 512 + 512],
                     i=bk_[:, ck_:ck_ + 512], b=BIAS[:, 1 * FC + h:1 * FC + h + 1]:
                     e.tensor_scalar(out=o, in0=i, scalar1=b, scalar2=None, op0=ALU.add),
                     waits=[pe_mm], inc="dve")
            bk_rd[bsl] = tev
            kb_toks.append(tev)
        ks_t = []
        for grp in range(2):
            bsl = 6 + grp
            bks, cks = bank(bsl)
            wv = kb_toks + [CONST_DMG, bk_rd[bsl]]
            for j in range(8):
                tt = grp * 8 + j
                st, sp = j == 0, j == 7
                op("tensor", lambda e, o=bks[:, cks + j * 64:cks + j * 64 + 64],
                   l=KB[:, tt * 128:tt * 128 + 128], r=WMB[:, 64:128], st=st, sp=sp:
                   e.matmul(o, l, r, start=st, stop=sp),
                   waits=wv if st else (), inc="pe" if sp else None)
            pe_mm = ("pe", cnt["pe"])
            scr1 = XT[:, grp * 512:grp * 512 + 512]
            scr2 = XT[:, 1024 + grp * 512:1024 + grp * 512 + 512]
            t1 = op("vector", lambda e, o=scr1, i=bks[:, cks:cks + 512]:
                    e.tensor_scalar_min(o, i, 0.0), waits=[pe_mm], inc="dve")
            t2 = op("scalar", lambda e, o=scr2, i=scr1:
                    e.activation(out=o, in_=i, func=AF.Exp), waits=[t1], inc="act")
            t3 = op("vector", lambda e, o=KT[:, grp * 512:grp * 512 + 512],
                    i=bks[:, cks:cks + 512], x2=scr2:
                    e.scalar_tensor_tensor(out=o, in0=i, scalar=0.0, in1=x2,
                                           op0=ALU.max, op1=ALU.add), waits=[t2], inc="dve")
            bk_rd[bsl] = t3
            ks_t.append(t3)
        vb_toks = []
        for ts in range(4):
            bsl = ts % 2
            bv_, cv_ = bank(bsl)
            wv = [tv_, dve_xn2, bk_rd[bsl]]
            for kc in range(FC):
                st, sp = kc == 0, kc == FC - 1
                op("tensor", lambda e, o=bv_[:, cv_:cv_ + 512],
                   l=PAN[pv][:, kc * 128:(kc + 1) * 128],
                   r=XN(kc)[:, ts * 512:ts * 512 + 512], st=st, sp=sp:
                   e.matmul(o, l, r, start=st, stop=sp),
                   waits=wv if st else (), inc="pe" if sp else None)
            pe_mm = ("pe", cnt["pe"])
            mark_panel(pv, pe_mm)
            tev = op("vector", lambda e, o=VB[:, ts * 512:ts * 512 + 512],
                     i=bv_[:, cv_:cv_ + 512], b=BIAS[:, 2 * FC + h:2 * FC + h + 1]:
                     e.tensor_scalar(out=o, in0=i, scalar1=b, scalar2=None, op0=ALU.add),
                     waits=[pe_mm], inc="dve")
            bk_rd[bsl] = tev
            vb_toks.append(tev)
        vt_toks = []
        for tt in range(16):
            bslot = 2 + tt % 2
            bxb, cxb = bankb(bslot)
            wv = [vb_toks[tt // 4], tones_v, bk_rd[bslot]]
            ttx = op("tensor", lambda e, o=bxb[:, cxb:cxb + 128],
                     i=VB[:, tt * 128:tt * 128 + 128]:
                     e.transpose(o, i, IDENTB[:, :]), waits=wv, inc="pe")
            tev = op("vector", lambda e, o=VT(tt)[:, 0:128], i=bxb[:, cxb:cxb + 128]:
                     e.tensor_copy(out=o, in_=i), waits=[ttx], inc="dve")
            bk_rd[bslot] = tev
            vt_toks.append(tev)
        bsl = 4 + (h % 2)
        bkv, ckv = bank(bsl)
        wv = ks_t + vt_toks + [bk_rd[bsl]]
        for tt in range(16):
            st, sp = tt == 0, tt == 15
            op("tensor", lambda e, o=bkv[0:64, ckv:ckv + 129],
               l=KT[:, tt * 64:tt * 64 + 64], r=VT(tt)[:, 0:129], st=st, sp=sp:
               e.matmul(o, l, r, start=st, stop=sp),
               waits=wv if st else (), inc="pe" if sp else None)
        pe_kv = ("pe", cnt["pe"])
        tkv = op("vector", lambda e,
                 o=KVP[base:base + 64, (h // 2) * 132:(h // 2) * 132 + 129],
                 i=bkv[0:64, ckv:ckv + 129]: e.tensor_copy(out=o, in_=i),
                 waits=[pe_kv], inc="dve")
        bk_rd[bsl] = tkv
        kv_toks.append(tkv)
        if h == 7:
            t_ccwA = dma("sync", cc_inA[:, :], KVP[:, 0:528], waits=kv_toks[:8])
            t_ccA = op("gpsimd", lambda e: e.collective_compute(
                "AllReduce", ALU.add, replica_groups=[[0, 1], [2, 3], [4, 5], [6, 7]],
                ins=[cc_inA.opt()], outs=[cc_outA.opt()]), waits=[t_ccwA], inc="ccs", amt=1)
            t_kvbA = dma("gpsimd", KVB[:, 0:528], cc_outA[:, :], waits=[t_ccA])
        if h == 15:
            t_ccwB = dma("sync", cc_inB[:, :], KVP[:, 528:1056], waits=kv_toks[8:])
            t_ccB = op("gpsimd", lambda e: e.collective_compute(
                "AllReduce", ALU.add, replica_groups=[[0, 1], [2, 3], [4, 5], [6, 7]],
                ins=[cc_inB.opt()], outs=[cc_outB.opt()]), waits=[t_ccwB], inc="cc2", amt=1)
            t_kvbB = dma("gpsimd", KVB[:, 528:1056], cc_outB[:, :], waits=[t_ccB])
    DVE_HEADS = NOW("dve")
    bk_rd = {i: DVE_HEADS for i in range(8)}
    ao_toks = []
    nrow_war = None
    for h in range(NH):
        base = 64 * (h % 2)
        qcol = (h // 2) * 2048
        kvcol = (h // 2) * 132
        t_kvb = t_kvbA if h < 8 else t_kvbB
        QS = AR2[base:base + 64, qcol:qcol + 2048]
        tmms = []
        for ts in range(4):
            bn_, cn_ = bank(4 + ts)
            tmms.append(op("tensor", lambda e, o=bn_[0:1, cn_:cn_ + 512],
                           l=KVB[base:base + 64, kvcol + 128:kvcol + 129],
                           r=QS[:, ts * 512:ts * 512 + 512]:
                           e.matmul(o, l, r, start=True, stop=True),
                           waits=[t_kvb, qs_done[h], bk_rd[4 + ts]], inc="pe"))
        op("vector", lambda e: e.tensor_scalar(
            out=NROWS[0:1, 0:1024], in0=PS[2][0:1, 0:1024], scalar1=EPS_AT,
            scalar2=None, op0=ALU.add), waits=[tmms[0], tmms[1], nrow_war], inc="dve")
        op("vector", lambda e: e.tensor_scalar(
            out=NROWS[0:1, 1024:2048], in0=PS[3][0:1, 0:1024], scalar1=EPS_AT,
            scalar2=None, op0=ALU.add), waits=[tmms[2], tmms[3]], inc="dve")
        trc = op("vector", lambda e: e.reciprocal(out=NROWS[0:1, 0:2048],
                                                  in_=NROWS[0:1, 0:2048]), inc="dve")
        qd_t = []
        for ts in range(4):
            bn_, cn_ = bank(4 + ts)
            tmb = op("tensor", lambda e, o=bn_[0:64, cn_:cn_ + 512],
                     r=NROWS[0:1, ts * 512:ts * 512 + 512]:
                     e.matmul(o, ONESf[0:1, 0:64], r, start=True, stop=True),
                     waits=[trc], inc="pe")
            if ts == 3:
                nrow_war = tmb
            tqd = op("vector", lambda e, o=HB[base:base + 64, ts * 512:ts * 512 + 512],
                     a=QS[:, ts * 512:ts * 512 + 512], b=bn_[0:64, cn_:cn_ + 512]:
                     e.tensor_tensor(out=o, in0=a, in1=b, op=ALU.mult),
                     waits=[tmb], inc="dve")
            bk_rd[4 + ts] = tqd
            qd_t.append(tqd)
        for ts in range(4):
            bsl = ts % 2
            bo_, co_ = bank(bsl)
            tmm = op("tensor", lambda e, o=bo_[:, co_:co_ + 512],
                     l=KVB[base:base + 64, kvcol:kvcol + 128],
                     r=HB[base:base + 64, ts * 512:ts * 512 + 512]:
                     e.matmul(o, l, r, start=True, stop=True),
                     waits=[qd_t[ts], t_kvb, bk_rd[bsl]], inc="pe")
            tev = op("vector", lambda e,
                     o=AR1b[:, h * TPC + ts * 512:h * TPC + ts * 512 + 512],
                     i=bo_[:, co_:co_ + 512]: e.tensor_copy(out=o, in_=i),
                     waits=[tmm], inc="dve")
            bk_rd[bsl] = tev
            ao_toks.append(tev)
    AO = lambda hh: AR1b[:, hh * TPC:(hh + 1) * TPC]
    ldx_extra = [NOW("act"), NOW("dve")]
    st2 = {}
    p2dst = x_state
    for oc in range(FC):
        pw, tw = load_panel(woT[oc, :, :])
        xst = XT[:, (oc % 2) * TPC:(oc % 2) * TPC + TPC]
        wvx = list(ldx_extra) if oc < 2 else [st2[oc - 2]]
        ldx = dma("sync", xst, asrc[oc, :, 0:TPC], waits=wvx)
        for ts in range(4):
            bsl = 2 + ts % 2
            bw_, cw_ = bank(bsl)
            wv2 = [tw, bk_rd[bsl]] + (ao_toks if (oc == 0 and ts == 0) else [ao_toks[-1]])
            for hh in range(NH):
                st, sp = hh == 0, hh == NH - 1
                op("tensor", lambda e, o=bw_[:, cw_:cw_ + 512],
                   l=PAN[pw][:, hh * 128:(hh + 1) * 128],
                   r=AO(hh)[:, ts * 512:ts * 512 + 512], st=st, sp=sp:
                   e.matmul(o, l, r, start=st, stop=sp),
                   waits=wv2 if st else (), inc="pe" if sp else None)
            pe_mm = ("pe", cnt["pe"])
            mark_panel(pw, pe_mm)
            tev = op("vector", lambda e, o=xst[:, ts * 512:ts * 512 + 512],
                     i=bw_[:, cw_:cw_ + 512], b=BIAS[:, 3 * FC + oc:3 * FC + oc + 1]:
                     e.scalar_tensor_tensor(out=o, in0=i, scalar=b, in1=o,
                                            op0=ALU.add, op1=ALU.add),
                     waits=[pe_mm, ldx], inc="dve")
            bk_rd[bsl] = tev
        st2[oc] = dma("sync", p2dst[oc, :, 0:TPC], xst, waits=[bk_rd[2], bk_rd[3]])
    add_entry([NOW("dms")])

    add_entry([NOW("dml")])
    emit_ffn_sparse(x_state, y_out, CAP3, 1, echo_write=False)

    op("sync", lambda e: e.nop(), waits=[NOW("dms"), NOW("dmg"), NOW("dml")])

    with nc.Block() as block:
        def mk(engname):
            own = SELF_SER.get(engname)

            def body(eng):
                lw = {}
                done = 0
                for fn, waits, inc, amt in prog[engname]:
                    if own is not None and done > 0:
                        eng.wait_ge(sems[own], done)
                    for (s, v) in waits:
                        if lw.get(s, 0) < v:
                            eng.wait_ge(sems[s], v)
                            lw[s] = v
                    instr = fn(eng)
                    if inc is not None:
                        instr.then_inc(sems[inc], amt)
                        if inc == own:
                            done += amt
            return body
        block.sync(mk("sync"))
        block.scalar(mk("scalar"))
        block.vector(mk("vector"))
        block.tensor(mk("tensor"))
        block.gpsimd(mk("gpsimd"))
    ctx.close()
    return nc


def _prep_inputs(kw):
    f32 = np.float32
    x = np.asarray(kw["L0_attn_hidden_states"], f32).reshape(B * S, D)
    w1 = np.concatenate([np.asarray(kw["L0_ffn_sw1"], f32),
                         np.asarray(kw["L0_ffn_w1"], f32)], axis=0)
    w2 = np.concatenate([np.asarray(kw["L0_ffn_sw2"], f32),
                         np.asarray(kw["L0_ffn_w2"], f32)], axis=0)
    w1R = np.ascontiguousarray(
        w1.reshape(NEX, FC, 128, HC, 128).transpose(0, 3, 2, 1, 4)).reshape(NEX, HC, 128, FC * 128)
    w2R = np.ascontiguousarray(
        w2.reshape(NEX, HC, 128, FC, 128).transpose(0, 3, 2, 1, 4)).reshape(NEX, FC, 128, HC * 128)

    def wt(a):
        at = np.ascontiguousarray(np.asarray(a, f32).T)
        return np.ascontiguousarray(
            at.reshape(FC, 128, FC, 128).transpose(2, 1, 0, 3)).reshape(FC, 128, FC * 128)

    wq = wt(kw["L0_attn_wq"]); wk = wt(kw["L0_attn_wk"])
    wv = wt(kw["L0_attn_wv"]); wo = wt(kw["L0_attn_wo"])
    gate = np.ascontiguousarray(
        np.asarray(kw["L0_ffn_gate_w"], f32).T.reshape(FC, 128, NE).transpose(1, 0, 2)
    ).reshape(128, FC * NE)
    biasC = np.ascontiguousarray(
        np.stack([np.asarray(kw["L0_attn_bq"], f32), np.asarray(kw["L0_attn_bk"], f32),
                  np.asarray(kw["L0_attn_bv"], f32), np.asarray(kw["L0_attn_bo"], f32)]
                 ).reshape(4, FC, 128).transpose(2, 0, 1)).reshape(128, 4 * FC)
    wmapT = np.ascontiguousarray(
        np.concatenate([np.asarray(kw["L0_attn_wmap_q"], f32).T,
                        np.asarray(kw["L0_attn_wmap_k"], f32).T], axis=1))
    ident = np.eye(128, dtype=f32)

    expbase = np.stack([np.arange(16, dtype=f32) * CAP1,
                        np.arange(16, dtype=f32) * CAP3], axis=1)
    shared = dict(w1R=w1R, w2R=w2R, wqT=wq, wkT=wk, wvT=wv, woT=wo, gateT=gate,
                  biasC=biasC, wmapT=wmapT, ident_in=ident, expbase_in=expbase)
    in_maps = []
    for c in range(NCORES):
        xc = np.ascontiguousarray(x[c * TPC:(c + 1) * TPC, :].T).reshape(FC, 128, TPC)
        m = {"x0": xc}
        m.update(shared)
        in_maps.append(m)
    return in_maps


def kernel(**kw):
    in_maps = _prep_inputs(kw)
    nc = build_program()
    res = run_bass_kernel_spmd(nc, in_maps, core_ids=list(range(NCORES)))
    outs = []
    for c in range(NCORES):
        yc = np.asarray(res.results[c]["y"]).reshape(D, TPC)
        outs.append(np.ascontiguousarray(yc.T))
    return np.concatenate(outs, axis=0).reshape(B, S, D).astype(np.float32)


# version 08 (final)
# speedup vs baseline: 1.3259x; 1.3259x over previous
import numpy as np
from contextlib import ExitStack

import concourse.bass as bass
import concourse.mybir as mybir
from concourse.bass_utils import run_bass_kernel_spmd
from concourse.bass_core_rust import IndirectOffsetOnAxis

F32 = mybir.dt.float32
BF16 = mybir.dt.bfloat16
AF = mybir.ActivationFunctionType
ALU = mybir.AluOpType
AX = mybir.AxisListType

B, S, D = 4, 4096, 2048
NCORES = 8
TPC = (B * S) // NCORES
H_FF = 1408
NE = 16
NEX = NE + 1
NH, DH, DM = 16, 128, 64
FC = D // 128
HC = H_FF // 128
TG = 1024
NG = TPC // TG
EPS_LN = 1e-5
EPS_AT = 1e-6

ENGS = ("sync", "scalar", "vector", "tensor", "gpsimd")

CAP1 = 384
CAP3 = 512
CAPM = 512


def build_program():
    nc = bass.Bass("TRN2", target_bir_lowering=False, debug=False)

    x0 = nc.dram_tensor("x0", [FC, 128, TPC], F32, kind="ExternalInput").ap()
    w1R = nc.dram_tensor("w1R", [NEX, HC, 128, FC * 128], F32, kind="ExternalInput").ap()
    w2R = nc.dram_tensor("w2R", [NEX, FC, 128, HC * 128], F32, kind="ExternalInput").ap()
    wqT = nc.dram_tensor("wqT", [FC, 128, FC * 128], F32, kind="ExternalInput").ap()
    wkT = nc.dram_tensor("wkT", [FC, 128, FC * 128], F32, kind="ExternalInput").ap()
    wvT = nc.dram_tensor("wvT", [FC, 128, FC * 128], F32, kind="ExternalInput").ap()
    woT = nc.dram_tensor("woT", [FC, 128, FC * 128], F32, kind="ExternalInput").ap()
    gateT = nc.dram_tensor("gateT", [128, FC * NE], F32, kind="ExternalInput").ap()
    biasC = nc.dram_tensor("biasC", [128, 4 * FC], F32, kind="ExternalInput").ap()
    wmapT = nc.dram_tensor("wmapT", [128, 2 * DM], F32, kind="ExternalInput").ap()
    ident_in = nc.dram_tensor("ident_in", [128, 128], F32, kind="ExternalInput").ap()
    expbase_in = nc.dram_tensor("expbase_in", [16, 2], F32, kind="ExternalInput").ap()

    y_out = nc.dram_tensor("y", [FC, 128, TPC], F32, kind="ExternalOutput").ap()
    x_state = nc.dram_tensor("x_state", [FC, 128, TPC], F32, kind="Internal").ap()
    cc_inA = nc.dram_tensor("cc_inA", [128, 528], F32, kind="Internal").ap()
    cc_outA = nc.dram_tensor("cc_outA", [128, 528], F32, kind="Internal").ap()
    cc_inB = nc.dram_tensor("cc_inB", [128, 528], F32, kind="Internal").ap()
    cc_outB = nc.dram_tensor("cc_outB", [128, 528], F32, kind="Internal").ap()
    exp_xn = nc.dram_tensor("exp_xn", [NE * CAPM, D], BF16, kind="Internal").ap()
    y_dram = nc.dram_tensor("y_dram", [NE * CAPM, D], F32, kind="Internal").ap()
    w1E = nc.dram_tensor("w1E", [NEX, HC, 128, FC * 128], BF16, kind="Internal").ap()
    w2E = nc.dram_tensor("w2E", [NEX, FC, 128, HC * 128], BF16, kind="Internal").ap()

    ctx = ExitStack()
    sb = lambda name, shape, dt: ctx.enter_context(nc.sbuf_tensor(name, shape, dt))

    AR1 = sb("ar1", [128, FC * TG], F32)
    AR1b = AR1.bitcast(BF16)
    AR2 = sb("ar2", [128, FC * TG], BF16)
    AR2f = AR2.bitcast(F32)
    HB = sb("hb", [128, 9728], BF16)
    HBf = HB.bitcast(F32)
    SHH = sb("shh", [128, HC * 512], BF16)
    hstage = lambda jc, ts: (SHH[:, jc * 512:jc * 512 + 512] if ts == 0
                             else HB[:, 4096 + jc * 512:4096 + jc * 512 + 512])
    PAN = [sb(f"pan{i}", [128, FC * 128], BF16) for i in range(4)]
    XT = sb("xts", [128, 2 * TPC], F32)
    S1 = sb("s1", [128, 2 * TPC], BF16)
    ROWS = sb("rows", [1, 5 * 512], F32)
    R_MU = ROWS[0:1, 0:512]
    R_VAR = ROWS[0:1, 512:1024]
    R_RR = ROWS[0:1, 1024:1536]
    R_NMR = ROWS[0:1, 1536:2048]
    R_SCR = ROWS[0:1, 2048:2560]
    NROWS = ROWS
    RT = sb("rt", [128, 128], F32)
    CMB = sb("cmb", [16, TG], F32)
    BIAS = sb("biasb", [128, 4 * FC], F32)
    GTE = sb("gte", [128, FC * NE], BF16)
    WMB = sb("wmb", [128, 2 * DM], BF16)
    IDENT = sb("identb", [128, 128], F32)
    ONESb = sb("onesb", [128, 4], BF16)
    ONESf = sb("onesf", [1, 128], F32)
    KVP = sb("kvp", [128, 8 * 132], F32)
    KVB = sb("kvb", [128, 8 * 132], BF16)
    IND0B = sb("ind0b", [128, 256], F32)
    INDAB = sb("indab", [128, 256], F32)
    CWTB = sb("cwtb", [128, 256], F32)
    RTS = sb("rts", [128, 64], F32)
    DSTIB = sb("dstib", [128, 32], mybir.dt.int32)
    CRB = sb("crb", [128, 32], F32)
    CNTB = sb("cntb", [16, 2], F32)
    EXPB = sb("expb", [16, 2], F32)
    IDENTB = sb("identbb", [128, 128], BF16)
    S1f = S1.bitcast(F32)
    XTb = XT.bitcast(BF16)

    PS = [ctx.enter_context(nc.psum_tensor(f"psb{i}", [128, 1024], F32)) for i in range(4)]
    PSb = [p.bitcast(BF16) for p in PS]

    def bank(i):
        return PS[i // 2], (i % 2) * 512

    def bankb(i):
        return PSb[i // 2], (i % 2) * 1024

    sems = {n: ctx.enter_context(nc.semaphore(n)) for n in
            ("dms", "dmg", "dml", "pe", "act", "dve", "ccs", "cc2")}
    cnt = {n: 0 for n in sems}
    prog = {e: [] for e in ENGS}
    OWNER = {"pe": "tensor", "act": "scalar", "dve": "vector"}

    pending = {e: [] for e in ENGS}

    def add_entry(toks):
        for e in ENGS:
            pending[e].extend(t for t in toks if t is not None)

    SELF_SER = {"vector": "dve", "scalar": "act"}

    def op(eng, fn, waits=(), inc=None, amt=1):
        w = [t for t in waits if t is not None]
        if pending[eng]:
            w = pending[eng] + w
            pending[eng] = []
        w = tuple((s, v) for (s, v) in w if v > 0 and OWNER.get(s) != eng)
        if inc is None and eng in SELF_SER:
            inc = SELF_SER[eng]
        tok = None
        if inc is not None:
            cnt[inc] += amt
            tok = (inc, cnt[inc])
        prog[eng].append((fn, w, inc, amt))
        return tok

    def dma(queue, out_ap, in_ap, waits=()):
        q = {"sync": "dms", "gpsimd": "dmg", "scalar": "dml"}[queue]
        return op(queue, lambda e, o=out_ap, i=in_ap: e.dma_start(out=o, in_=i),
                  waits=waits, inc=q, amt=16)

    NOW = lambda s: (s, cnt[s])

    dma("sync", BIAS[:, :], biasC[:, :])
    dma("sync", IDENT[:, :], ident_in[:, :])
    dma("gpsimd", GTE[:, :], gateT[:, :])
    dma("gpsimd", WMB[:, :], wmapT[:, :])
    dma("sync", EXPB[:, :], expbase_in[:, :])
    dma("gpsimd", IDENTB[:, :], ident_in[:, :])
    op("vector", lambda e: e.memset(ONESb[:, :], 1.0), inc="dve")
    op("vector", lambda e: e.memset(ONESf[:, :], 1.0), inc="dve")
    CONST_DMS = NOW("dms")
    CONST_DMG = NOW("dmg")
    CONST_DVE = NOW("dve")

    pan_state = [{"r": None, "e": None} for _ in range(4)]
    pan_ptr = {"i": 0}

    def load_panel(src_ap, echo_to=None):
        i = pan_ptr["i"] % 4
        pan_ptr["i"] += 1
        tok = dma("gpsimd", PAN[i][:, 0:src_ap.shape[-1]], src_ap,
                  waits=[pan_state[i]["r"], pan_state[i]["e"]])
        if echo_to is not None:
            pan_state[i]["e"] = dma("scalar", echo_to,
                                    PAN[i][:, 0:src_ap.shape[-1]], waits=[tok])
        return i, tok

    def mark_panel(i, pe_tok):
        pan_state[i]["r"] = pe_tok

    def ln_seg_chain(bsum_i, bsq_i, pe_stats, war_pe):
        bsum, csum = bank(bsum_i)
        bsq_, csq = bank(bsq_i)
        op("vector", lambda e: e.tensor_scalar(out=R_MU, in0=bsum[0:1, csum:csum + 512],
                                               scalar1=1.0 / D, scalar2=None, op0=ALU.mult),
           waits=[pe_stats, war_pe])
        op("vector", lambda e: e.tensor_scalar(out=R_VAR, in0=bsq_[0:1, csq:csq + 512],
                                               scalar1=1.0 / D, scalar2=EPS_LN,
                                               op0=ALU.mult, op1=ALU.add))
        op("vector", lambda e: e.tensor_tensor(out=R_SCR, in0=R_MU, in1=R_MU, op=ALU.mult))
        tvar = op("vector", lambda e: e.tensor_tensor(out=R_VAR, in0=R_VAR, in1=R_SCR,
                                                      op=ALU.subtract), inc="dve")
        tstd = op("scalar", lambda e: e.activation(out=R_VAR, in_=R_VAR, func=AF.Sqrt),
                  waits=[tvar], inc="act")
        trec = op("vector", lambda e: e.reciprocal(out=R_RR, in_=R_VAR), waits=[tstd], inc="dve")
        tnmr = op("vector", lambda e: e.scalar_tensor_tensor(
            out=R_NMR, in0=R_MU, scalar=-1.0, in1=R_RR, op0=ALU.mult, op1=ALU.mult), inc="dve")
        op("tensor", lambda e: e.matmul(bsum[:, csum:csum + 512], ONESf[0:1, :], R_RR,
                                        start=True, stop=True), waits=[trec, CONST_DVE])
        tok = op("tensor", lambda e: e.matmul(bsq_[:, csq:csq + 512], ONESf[0:1, :], R_NMR,
                                              start=True, stop=True), waits=[tnmr], inc="pe")
        return tok

    pairs = [(0, 1), (0, 2), (0, 3), (1, 2), (1, 3), (2, 3)]

    def emit_ffn_sparse(src, dst, cap, expcol, echo_write):
        IOA = IndirectOffsetOnAxis
        ci = cap // 128

        fs = {
            "pe_stats_par": {0: None, 1: None},
            "stats_war": NOW("dve"),
            "scat_toks": [],
            "bank_rd": {i: None for i in range(8)},
            "t_cnt": None,
            "prev_tev": None,
        }
        bank_rd = fs["bank_rd"]

        def emit_front_ln(g):
            t0 = g * TG
            ld = []
            for fc in range(FC):
                ld.append(dma("sync", AR1[:, fc * TG:(fc + 1) * TG],
                              src[fc, :, t0:t0 + TG]))
            for fc in range(FC):
                xg = AR1[:, fc * TG:(fc + 1) * TG]
                xb = S1[:, (fc % 2) * TG:(fc % 2) * TG + TG]
                sq = S1[:, 2 * TG + (fc % 2) * TG:2 * TG + (fc % 2) * TG + TG]
                tcast = op("vector", lambda e, o=xb, i=xg: e.tensor_copy(out=o, in_=i),
                           waits=[ld[fc], fs["pe_stats_par"][fc % 2]])
                tsq = op("scalar", lambda e, o=sq, i=xb: e.activation(out=o, in_=i, func=AF.Square),
                         waits=[tcast])
                first, last = fc == 0, fc == FC - 1
                for seg in range(2):
                    bsum, csum = bank(4 + 2 * seg)
                    bsq_, csq = bank(5 + 2 * seg)
                    wmm = [tcast, tsq] + ([fs["stats_war"]] if first else [])
                    op("tensor", lambda e, o=bsum[0:1, csum:csum + 512], l=ONESb[:, 0:1],
                       r=xb[:, seg * 512:seg * 512 + 512], st=first, sp=last:
                       e.matmul(o, l, r, start=st, stop=sp), waits=wmm if first else [tcast, tsq])
                    ptok = op("tensor", lambda e, o=bsq_[0:1, csq:csq + 512], l=ONESb[:, 1:2],
                              r=sq[:, seg * 512:seg * 512 + 512], st=first, sp=last:
                              e.matmul(o, l, r, start=st, stop=sp), inc="pe")
                fs["pe_stats_par"][fc % 2] = ptok
            pe_stats = NOW("pe")
            pe_bc = None
            for seg in range(2):
                pe_bc = ln_seg_chain(4 + 2 * seg, 5 + 2 * seg, pe_stats, pe_bc)
            for fc in range(FC):
                for seg in range(2):
                    br, cr = bank(4 + 2 * seg)
                    bn, cn = bank(5 + 2 * seg)
                    xgs = AR1[:, fc * TG + seg * 512:fc * TG + seg * 512 + 512]
                    tts = XT[:, (fc % 4) * 512:(fc % 4) * 512 + 512]
                    xns = AR2[:, fc * TG + seg * 512:fc * TG + seg * 512 + 512]
                    op("vector", lambda e, o=tts, a=xgs, b=br[:, cr:cr + 512]:
                       e.tensor_tensor(out=o, in0=a, in1=b, op=ALU.mult), waits=[pe_bc])
                    op("vector", lambda e, o=xns, a=tts, b=bn[:, cn:cn + 512]:
                       e.tensor_tensor(out=o, in0=a, in1=b, op=ALU.add))
            return NOW("dve"), pe_stats

        def routing_scatter_gen(g, dve_xn, pe_stats):
            gsl = slice(g * 128, (g + 1) * 128)
            rt_toks = []
            bs, cs = bank(6)
            for tt in range(8):
                for kc in range(FC):
                    st, sp = kc == 0, kc == FC - 1
                    op("tensor", lambda e, o=bs[0:128, cs + tt * 16:cs + tt * 16 + 16],
                       l=AR2[:, kc * TG + tt * 128:kc * TG + tt * 128 + 128],
                       r=GTE[:, kc * NE:(kc + 1) * NE], st=st, sp=sp:
                       e.matmul(o, l, r, start=st, stop=sp),
                       waits=[dve_xn, CONST_DMG, fs.get("sig_prev")] if (st and tt == 0)
                       else ([dve_xn] if st else ()),
                       inc="pe" if (sp and tt == 7) else None)
                if tt % 2 == 1:
                    yield
            pe_gate = ("pe", cnt["pe"])
            sig = op("scalar", lambda e: e.activation(
                out=RT[:, 0:128], in_=bs[0:128, cs:cs + 128], func=AF.Sigmoid),
                waits=[pe_gate])
            fs["sig_prev"] = sig
            ps = lambda pi: XT[:, pi * 32:(pi + 1) * 32]
            gsb = XT[:, 192:224]
            pmin = lambda pi: XT[:, 224 + pi * 8:232 + pi * 8]
            thr = XT[:, 272:280]
            sel = XT[:, 288:320]
            mbuf = XT[:, 320:448]
            mx1 = XT[:, 448:456]
            mx2 = XT[:, 456:464]
            den = XT[:, 464:472]
            m2 = XT[:, 512:640]
            tmask = XT[:, 640:768]
            r3 = lambda ap: ap.rearrange("p (t e) -> p t e", e=16)
            bc = lambda ap: ap.unsqueeze(-1).broadcast_to((128, 8, 16))
            for pi, (a, b) in enumerate(pairs):
                op("vector", lambda e, a_=a, b_=b, pi_=pi: e.tensor_tensor(
                    out=ps(pi_), in0=RT[:, a_:128:4], in1=RT[:, b_:128:4], op=ALU.add),
                    waits=[sig] if pi == 0 else ())
            op("vector", lambda e: e.tensor_tensor(out=gsb, in0=ps(0), in1=ps(1), op=ALU.max))
            for pi in range(2, 6):
                op("vector", lambda e, pi_=pi: e.tensor_tensor(
                    out=gsb, in0=gsb, in1=ps(pi_), op=ALU.max))
            for pi, (a, b) in enumerate(pairs):
                op("vector", lambda e, a_=a, b_=b, pi_=pi: e.tensor_tensor(
                    out=pmin(pi_), in0=XT[:, 192 + a_:224:4], in1=XT[:, 192 + b_:224:4],
                    op=ALU.min))
            op("vector", lambda e: e.tensor_tensor(out=thr, in0=pmin(0), in1=pmin(1), op=ALU.max))
            for pi in range(2, 6):
                op("vector", lambda e, pi_=pi: e.tensor_tensor(
                    out=thr, in0=thr, in1=pmin(pi_), op=ALU.max))
            for g4 in range(4):
                op("vector", lambda e, g_=g4: e.tensor_tensor(
                    out=XT[:, 288 + g_:320:4], in0=XT[:, 192 + g_:224:4], in1=thr,
                    op=ALU.is_ge))
            for a in range(4):
                op("vector", lambda e, a_=a: e.tensor_tensor(
                    out=XT[:, 320 + a_:448:4], in0=RT[:, a_:128:4], in1=sel, op=ALU.mult))
            op("vector", lambda e: e.tensor_reduce(out=mx1.unsqueeze(-1), in_=r3(mbuf),
                                                   op=ALU.max, axis=AX.X))
            op("vector", lambda e: e.tensor_tensor(out=r3(IND0B[:, gsl]), in0=r3(mbuf),
                                                   in1=bc(mx1), op=ALU.is_ge))
            op("vector", lambda e: e.scalar_tensor_tensor(
                out=m2, in0=IND0B[:, gsl], scalar=-1e30, in1=mbuf,
                op0=ALU.mult, op1=ALU.add))
            op("vector", lambda e: e.tensor_reduce(out=mx2.unsqueeze(-1), in_=r3(m2),
                                                   op=ALU.max, axis=AX.X))
            op("vector", lambda e: e.tensor_tensor(out=r3(INDAB[:, gsl]), in0=r3(mbuf),
                                                   in1=bc(mx2), op=ALU.is_ge))
            op("vector", lambda e: e.tensor_tensor(out=den, in0=mx1, in1=mx2, op=ALU.add))
            op("vector", lambda e: e.tensor_scalar(out=den, in0=den, scalar1=1e-20,
                                                   scalar2=None, op0=ALU.add))
            op("vector", lambda e: e.reciprocal(out=den, in_=den))
            op("vector", lambda e: e.tensor_tensor(out=tmask, in0=mbuf, in1=INDAB[:, gsl],
                                                   op=ALU.mult))
            tcw_all = op("vector", lambda e: e.tensor_tensor(
                out=r3(CWTB[:, gsl]), in0=r3(tmask), in1=bc(den), op=ALU.mult))
            yield
            for tt in range(8):
                bt, ct = bank(7)
                ttr = op("tensor", lambda e, o=bt[0:16, ct:ct + 128],
                         i=CWTB[:, g * 128 + tt * 16:g * 128 + tt * 16 + 16]:
                         e.transpose(o, i, IDENT[:, :]),
                         waits=[tcw_all, CONST_DMS, fs["prev_tev"]], inc="pe")
                tev = op("vector", lambda e, o=CMB[0:16, tt * 128:tt * 128 + 128],
                         i=bt[0:16, ct:ct + 128]: e.tensor_copy(out=o, in_=i),
                         waits=[ttr])
                fs["prev_tev"] = tev
                rt_toks.append(tev)
                if tt % 2 == 1:
                    yield
            selv = XT[0:16, 2048:3072]
            dstf = XT[0:16, 3072:4096]
            cur = S1f[0:16, 0:1024]
            oth = S1f[0:16, 1024:2048]
            op("vector", lambda e: e.tensor_scalar(out=selv, in0=CMB[0:16, :], scalar1=0.0,
                                                   scalar2=None, op0=ALU.is_gt),
               waits=rt_toks + [pe_stats])
            op("vector", lambda e: e.tensor_copy(out=cur, in_=selv))
            sh = 1
            while sh < 1024:
                op("vector", lambda e, a=oth[0:16, 0:sh], b=cur[0:16, 0:sh]:
                   e.tensor_copy(out=a, in_=b))
                op("vector", lambda e, o=oth[0:16, sh:1024], a=cur[0:16, sh:1024],
                   b=cur[0:16, 0:1024 - sh]:
                   e.tensor_tensor(out=o, in0=a, in1=b, op=ALU.add))
                cur, oth = oth, cur
                sh *= 2
            op("vector", lambda e, c=cur: e.tensor_tensor(out=dstf, in0=c, in1=selv,
                                                          op=ALU.subtract))
            if g == 0:
                fs["t_cnt"] = op("vector", lambda e, c=cur: e.tensor_scalar(
                    out=CNTB[0:16, 0:1], in0=c[0:16, 1023:1024],
                    scalar1=float(cap - 1), scalar2=None, op0=ALU.min))
            else:
                op("vector", lambda e: e.tensor_scalar(out=dstf, in0=dstf,
                                                       scalar1=CNTB[0:16, 0:1], scalar2=None,
                                                       op0=ALU.add), waits=[fs["t_cnt"]])
            op("vector", lambda e: e.tensor_scalar(out=dstf, in0=dstf,
                                                   scalar1=float(cap - 1), scalar2=None,
                                                   op0=ALU.min))
            t_dstf = op("vector", lambda e: e.tensor_scalar(
                out=dstf, in0=dstf, scalar1=EXPB[0:16, expcol:expcol + 1], scalar2=None,
                op0=ALU.add), waits=[CONST_DMS])
            yield
            bt, ct = bank(7)
            tr2_toks = []
            for tt in range(8):
                tr2_toks.append(op("tensor", lambda e, o=bt[0:128, ct + tt * 16:ct + tt * 16 + 16],
                                   i=dstf[0:16, tt * 128:tt * 128 + 128]:
                                   e.transpose(o, i, IDENT[0:16, 0:16]),
                                   waits=[t_dstf, bank_rd[7], fs["prev_tev"]] if tt == 0 else (),
                                   inc="pe" if tt == 7 else None))
            dst_tm = XT[:, 768:896]
            rank1 = XT[:, 896:1024]
            prod = XT[:, 1024:1152]
            slot0 = XT[:, 1280:1288]
            slot1 = XT[:, 1288:1296]
            cw0 = XT[:, 1296:1304]
            cw1 = XT[:, 1304:1312]
            r3b = lambda ap: ap.rearrange("p (t e) -> p t e", e=16)
            tdstt = op("vector", lambda e: e.tensor_copy(out=dst_tm, in_=bt[0:128, ct:ct + 128]),
                       waits=[tr2_toks[-1]])
            bank_rd[7] = tdstt
            op("vector", lambda e: e.tensor_tensor(
                out=rank1, in0=INDAB[:, gsl], in1=IND0B[:, gsl], op=ALU.subtract))
            op("vector", lambda e: e.tensor_tensor(out=prod, in0=dst_tm, in1=IND0B[:, gsl],
                                                   op=ALU.mult))
            op("vector", lambda e: e.tensor_reduce(out=slot0.unsqueeze(-1), in_=r3b(prod),
                                                   op=ALU.add, axis=AX.X))
            op("vector", lambda e: e.tensor_tensor(out=prod, in0=dst_tm, in1=rank1,
                                                   op=ALU.mult))
            op("vector", lambda e: e.tensor_reduce(out=slot1.unsqueeze(-1), in_=r3b(prod),
                                                   op=ALU.add, axis=AX.X))
            op("vector", lambda e: e.tensor_tensor(out=prod, in0=CWTB[:, gsl],
                                                   in1=IND0B[:, gsl], op=ALU.mult))
            op("vector", lambda e: e.tensor_reduce(out=cw0.unsqueeze(-1), in_=r3b(prod),
                                                   op=ALU.add, axis=AX.X))
            op("vector", lambda e: e.tensor_tensor(out=prod, in0=CWTB[:, gsl],
                                                   in1=rank1, op=ALU.mult))
            op("vector", lambda e: e.tensor_reduce(out=cw1.unsqueeze(-1), in_=r3b(prod),
                                                   op=ALU.add, axis=AX.X))
            c0g = g * 16
            op("vector", lambda e: e.tensor_copy(out=DSTIB[:, c0g:c0g + 16:2], in_=slot0))
            op("vector", lambda e: e.tensor_copy(out=DSTIB[:, c0g + 1:c0g + 16:2], in_=slot1))
            op("vector", lambda e: e.tensor_copy(out=CRB[:, c0g:c0g + 16:2], in_=cw0))
            op("vector", lambda e: e.tensor_copy(out=CRB[:, c0g + 1:c0g + 16:2], in_=cw1))
            t_rank = NOW("dve")
            yield
            for tt in range(TG // 128):
                gi = g * 8 + tt
                xbuf = HB[:, (tt % 2) * 2048:(tt % 2) * 2048 + 2048]
                ev_toks = []
                for kq in range(4):
                    bslot = 4 + kq % 2
                    bxb, cxb = bankb(bslot)
                    tq_toks = []
                    for kk in range(4):
                        kc = kq * 4 + kk
                        w = [dve_xn, CONST_DMG, bank_rd[bslot]] if kk == 0 else ()
                        tq_toks.append(op("tensor", lambda e,
                                          o=bxb[:, cxb + kk * 128:cxb + kk * 128 + 128],
                                          i=AR2[:, kc * TG + tt * 128:kc * TG + tt * 128 + 128]:
                                          e.transpose(o, i, IDENTB[:, :]), waits=w,
                                          inc="pe" if kk == 3 else None))
                    wv2 = [tq_toks[-1]]
                    if kq == 0 and len(fs["scat_toks"]) >= 2:
                        wv2.append(fs["scat_toks"][-2])
                    tev2 = op("vector", lambda e, o=xbuf[:, kq * 512:(kq + 1) * 512],
                              i=bxb[:, cxb:cxb + 512]: e.tensor_copy(out=o, in_=i),
                              waits=wv2)
                    bank_rd[bslot] = tev2
                    ev_toks.append(tev2)
                for r in range(2):
                    fs["scat_toks"].append(op("gpsimd", lambda e, c=gi * 2 + r, xb_=xbuf:
                                              e.indirect_dma_start(
                                                  exp_xn[:, :], IOA(DSTIB[:, c:c + 1], 0),
                                                  xb_[:, :], None),
                                              waits=[t_rank, ev_toks[-1]], inc="dmg", amt=16))
                yield

        def shared_gen(g, dve_xn, sh_toks):
            if echo_write and g == 0:
                ld_w1 = lambda jc: load_panel(w1R[0, jc, :, :], echo_to=w1E[0, jc, :, :])
                ld_w2 = lambda dc: load_panel(w2R[0, dc, :, :], echo_to=w2E[0, dc, :, :])
                bar = None
            else:
                pending["gpsimd"].append(("dml", cnt["dml"]))
                ld_w1 = lambda jc: load_panel(w1E[0, jc, :, :])
                ld_w2 = lambda dc: load_panel(w2E[0, dc, :, :])
            for jc in range(HC):
                p_i, p_tok = ld_w1(jc)
                for ts in range(2):
                    bsl = (2 * jc + ts) % 2
                    bh, ch = bank(bsl)
                    wvm = [p_tok, dve_xn, bank_rd[bsl]]
                    for kc in range(FC):
                        st, sp = kc == 0, kc == FC - 1
                        op("tensor", lambda e, o=bh[:, ch:ch + 512],
                           l=PAN[p_i][:, kc * 128:(kc + 1) * 128],
                           r=AR2[:, kc * TG + ts * 512:kc * TG + ts * 512 + 512],
                           st=st, sp=sp: e.matmul(o, l, r, start=st, stop=sp),
                           waits=wvm if st else (), inc="pe" if sp else None)
                    pe_mm = ("pe", cnt["pe"])
                    mark_panel(p_i, pe_mm)
                    tact = op("scalar", lambda e, o=hstage(jc, ts),
                              i=bh[:, ch:ch + 512]: e.activation(out=o, in_=i, func=AF.Silu),
                              waits=[pe_mm])
                    bank_rd[bsl] = tact
                yield
            silu_last = tact
            for dc in range(FC):
                p_i, p_tok = ld_w2(dc)
                for ts in range(2):
                    bsl = 2 + (2 * dc + ts) % 2
                    by, cy = bank(bsl)
                    wvm = [p_tok, silu_last, bank_rd[bsl]]
                    for hc in range(HC):
                        st, sp = hc == 0, hc == HC - 1
                        op("tensor", lambda e, o=by[:, cy:cy + 512],
                           l=PAN[p_i][:, hc * 128:(hc + 1) * 128],
                           r=hstage(hc, ts), st=st, sp=sp:
                           e.matmul(o, l, r, start=st, stop=sp),
                           waits=wvm if st else (), inc="pe" if sp else None)
                    pe_mm = ("pe", cnt["pe"])
                    mark_panel(p_i, pe_mm)
                    xgs = AR1[:, dc * TG + ts * 512:dc * TG + ts * 512 + 512]
                    tacc = op("vector", lambda e, o=xgs, a=by[:, cy:cy + 512]:
                              e.scalar_tensor_tensor(out=o, in0=a, scalar=0.5, in1=o,
                                                     op0=ALU.mult, op1=ALU.add),
                              waits=[pe_mm, sh_toks[dc]])
                    bank_rd[bsl] = tacc
                    sh_toks[dc] = tacc
                yield

        for g in range(NG):
            t0 = g * TG
            dve_xn, pe_stats = emit_front_ln(g)
            sh_toks = {fc: None for fc in range(FC)}
            rg = routing_scatter_gen(g, dve_xn, pe_stats)
            sg = shared_gen(g, dve_xn, sh_toks)
            r_alive = s_alive = True
            while r_alive or s_alive:
                if r_alive:
                    try:
                        next(rg)
                    except StopIteration:
                        r_alive = False
                if s_alive:
                    try:
                        next(sg)
                    except StopIteration:
                        s_alive = False
            for fc in range(FC):
                dma("sync", dst[fc, :, t0:t0 + TG], AR1[:, fc * TG:(fc + 1) * TG],
                    waits=[sh_toks[fc]])
            add_entry([NOW("dms")])

        AR1f = AR1
        yr0 = (8192 + HC * cap + 1) // 2
        yr0 = (yr0 + 127) & ~127
        YALL = lambda i: AR1f[:, yr0 + i * 2048:yr0 + (i + 1) * 2048]
        assert yr0 + ci * 2048 <= FC * TG
        intr_pe = {}
        m1_done = None
        ystore_done = {}
        yrow_last = {}
        for ei in range(NE):
            eb = ei * cap
            XET = XTb[:, 0:ci * 2048] if ei % 2 == 0 else HB[:, 0:ci * 2048]
            wv = [fs["scat_toks"][-1]] if ei < 2 else [intr_pe[ei - 2]]
            if ei < 2:
                wv.append(NOW("dve"))
            t_ld = dma("gpsimd", XET.rearrange("p (i c) -> p i c", c=D),
                       exp_xn[eb:eb + cap, :].rearrange("(i p) c -> p i c", p=128),
                       waits=wv)
            first_evac = True
            ar1k = AR1b[:, 0:FC * cap].rearrange("p (k c) -> p k c", c=cap)
            for i in range(ci):
                for kq in range(4):
                    bslot = 4 + kq % 2
                    bxb, cxb = bankb(bslot)
                    tq_last = None
                    for kk in range(4):
                        kc = kq * 4 + kk
                        w = [t_ld, bank_rd[bslot]] if kk == 0 else ()
                        tq_last = op("tensor", lambda e,
                                     o=bxb[:, cxb + kk * 128:cxb + kk * 128 + 128],
                                     i=XET[:, i * 2048 + kc * 128:i * 2048 + kc * 128 + 128]:
                                     e.transpose(o, i, IDENTB[:, :]), waits=w,
                                     inc="pe" if kk == 3 else None)
                    wev = [tq_last]
                    if first_evac:
                        wev.append(m1_done)
                        if ei == 0:
                            wev.append(NOW("dms"))
                        first_evac = False
                    tev2 = op("vector", lambda e,
                              o=ar1k[:, kq * 4:kq * 4 + 4, i * 128:i * 128 + 128],
                              i=bxb[:, cxb:cxb + 512].rearrange("p (k c) -> p k c", c=128):
                              e.tensor_copy(out=o, in_=i), waits=wev)
                    bank_rd[bslot] = tev2
            intr_pe[ei] = ("pe", cnt["pe"])
            t_intr = NOW("dve")
            silu_last = None
            for jc in range(HC):
                if echo_write:
                    p_i, p_tok = load_panel(w1R[ei + 1, jc, :, :],
                                            echo_to=w1E[ei + 1, jc, :, :])
                else:
                    p_i, p_tok = load_panel(w1E[ei + 1, jc, :, :])
                bsl = jc % 2
                bh, ch = bank(bsl)
                wvm = [p_tok, t_intr, bank_rd[bsl]]
                for kc in range(FC):
                    st, sp = kc == 0, kc == FC - 1
                    op("tensor", lambda e, o=bh[:, ch:ch + cap],
                       l=PAN[p_i][:, kc * 128:(kc + 1) * 128],
                       r=AR1b[:, kc * cap:kc * cap + cap], st=st, sp=sp:
                       e.matmul(o, l, r, start=st, stop=sp),
                       waits=wvm if st else (), inc="pe" if sp else None)
                pe_mm = ("pe", cnt["pe"])
                mark_panel(p_i, pe_mm)
                tact = op("scalar", lambda e, o=AR1b[:, 8192 + jc * cap:8192 + jc * cap + cap],
                          i=bh[:, ch:ch + cap]: e.activation(out=o, in_=i, func=AF.Silu),
                          waits=[pe_mm])
                bank_rd[bsl] = tact
                silu_last = tact
            m1_done = ("pe", cnt["pe"])
            for dc in range(FC):
                if echo_write:
                    p_i, p_tok = load_panel(w2R[ei + 1, dc, :, :],
                                            echo_to=w2E[ei + 1, dc, :, :])
                else:
                    p_i, p_tok = load_panel(w2E[ei + 1, dc, :, :])
                bsl = 2 + dc % 2
                by, cy = bank(bsl)
                wvm = [p_tok, silu_last, bank_rd[bsl]]
                for hc in range(HC):
                    st, sp = hc == 0, hc == HC - 1
                    op("tensor", lambda e, o=by[:, cy:cy + cap],
                       l=PAN[p_i][:, hc * 128:(hc + 1) * 128],
                       r=AR1b[:, 8192 + hc * cap:8192 + hc * cap + cap], st=st, sp=sp:
                       e.matmul(o, l, r, start=st, stop=sp),
                       waits=wvm if st else (), inc="pe" if sp else None)
                pe_mm = ("pe", cnt["pe"])
                mark_panel(p_i, pe_mm)
                yfc = S1f[:, (dc % 2) * cap:(dc % 2) * cap + cap]
                wve = [pe_mm]
                if ei == 0 and dc < 2:
                    wve.append(NOW("dve"))
                tev3 = op("vector", lambda e, o=yfc, i=by[:, cy:cy + cap]:
                          e.tensor_copy(out=o, in_=i), waits=wve)
                bank_rd[bsl] = tev3
                for i in range(ci):
                    bslot = 6 + i % 2
                    byt, cyt = bank(bslot)
                    w = [tev3, bank_rd[bslot]]
                    tty = op("tensor", lambda e, o=byt[:, cyt:cyt + 128],
                             i=yfc[:, i * 128:i * 128 + 128]:
                             e.transpose(o, i, IDENT[:, :]), waits=w, inc="pe")
                    wev4 = [tty]
                    if dc == 0:
                        wev4.append(ystore_done.get(i))
                    tev4 = op("vector", lambda e, o=YALL(i)[:, dc * 128:dc * 128 + 128],
                              i=byt[:, cyt:cyt + 128]: e.tensor_copy(out=o, in_=i),
                              waits=wev4)
                    bank_rd[bslot] = tev4
                    yrow_last[i] = tev4
            for i in range(ci):
                ystore_done[i] = dma("sync",
                                     y_dram[eb + i * 128:eb + (i + 1) * 128, :],
                                     YALL(i)[:, :], waits=[yrow_last[i]])
        PE_EXP = NOW("pe")
        Y_DMS = NOW("dms")
        DVE_EXP = NOW("dve")
        gbA = lambda q: XT[:, q * 2048:(q + 1) * 2048]
        gbB = lambda q: HBf[:, q * 2048:(q + 1) * 2048]
        cbv = lambda q: AR2f[:, q * 2048:(q + 1) * 2048]
        g_tok = {}
        cb_done = {0: None, 1: None}
        tr_done = {0: PE_EXP, 1: PE_EXP}

        def issue_gather(fi, extra=()):
            q = fi % 2
            w0 = list(extra) + [cb_done[q]]
            t0g = op("gpsimd", lambda e, c=2 * fi, q_=q: e.indirect_dma_start(
                gbA(q_), None, y_dram[:, :], IOA(DSTIB[:, c:c + 1], 0)),
                waits=w0, inc="dmg", amt=16)
            t1g = op("gpsimd", lambda e, c=2 * fi + 1, q_=q: e.indirect_dma_start(
                gbB(q_), None, y_dram[:, :], IOA(DSTIB[:, c:c + 1], 0)),
                inc="dmg", amt=16)
            g_tok[fi] = (t0g, t1g)

        issue_gather(0, extra=[Y_DMS, PE_EXP, DVE_EXP])
        issue_gather(1)
        ld2 = {}
        st_prev = {}
        for g in range(NG):
            t0c = g * TG
            acc_toks = {fc: None for fc in range(FC)}
            for fc in range(FC):
                w = [st_prev.get(fc)]
                if g == 0 and fc == 0:
                    w += [PE_EXP, DVE_EXP, Y_DMS]
                ld2[fc] = dma("sync", AR1[:, fc * TG:(fc + 1) * TG],
                              dst[fc, :, t0c:t0c + TG], waits=w)
            LD_ALL = NOW("dms")
            for tt in range(8):
                fi = g * 8 + tt
                q = fi % 2
                tA, tB = g_tok[fi]
                op("vector", lambda e, q_=q, c=2 * fi: e.tensor_scalar(
                    out=cbv(q_), in0=gbA(q_), scalar1=CRB[:, c:c + 1], scalar2=None,
                    op0=ALU.mult), waits=[tA, tr_done[q]] + ([LD_ALL] if tt == 0 else []))
                c1 = op("vector", lambda e, q_=q, c=2 * fi + 1: e.scalar_tensor_tensor(
                    out=cbv(q_), in0=gbB(q_), scalar=CRB[:, c:c + 1], in1=cbv(q_),
                    op0=ALU.mult, op1=ALU.add), waits=[tB])
                cb_done[q] = c1
                if fi + 2 < NG * 8:
                    issue_gather(fi + 2)
                for fc in range(FC):
                    bslot = 6 + fc % 2
                    bat, cat = bank(bslot)
                    tta = op("tensor", lambda e, o=bat[:, cat:cat + 128], q_=q, f_=fc:
                             e.transpose(o, cbv(q_)[:, f_ * 128:f_ * 128 + 128], IDENT[:, :]),
                             waits=[c1, bank_rd[bslot]], inc="pe")
                    tacc = op("vector", lambda e, o=AR1[:, fc * TG + tt * 128:
                                                       fc * TG + tt * 128 + 128],
                              i=bat[:, cat:cat + 128]:
                              e.scalar_tensor_tensor(out=o, in0=i, scalar=0.5, in1=o,
                                                     op0=ALU.mult, op1=ALU.add),
                              waits=[tta, ld2[fc], acc_toks[fc]])
                    bank_rd[bslot] = tacc
                    acc_toks[fc] = tacc
                tr_done[q] = ("pe", cnt["pe"])
            st_tok = NOW("dve")
            for fc in range(FC):
                st_prev[fc] = dma("sync", dst[fc, :, t0c:t0c + TG],
                                  AR1[:, fc * TG:(fc + 1) * TG], waits=[st_tok])
            add_entry([NOW("dms"), NOW("pe")])
        return

    emit_ffn_sparse(x0, x_state, CAP1, 0, echo_write=True)
    add_entry([NOW("dms")])

    asrc = x_state
    stok = {0: None, 1: None}
    cast_prev = {0: None, 1: None}
    for fc in range(FC):
        par = fc % 2
        xtp = XT[:, par * TPC:par * TPC + TPC]
        ldt = dma("sync", xtp, asrc[fc, :, 0:TPC], waits=[cast_prev[par]])
        xb = S1[:, 0:TPC] if par == 0 else SHH[:, 0:TPC]
        sq = S1[:, TPC:2 * TPC] if par == 0 else SHH[:, TPC:2 * TPC]
        tcast = op("vector", lambda e, o=xb, i=xtp: e.tensor_copy(out=o, in_=i),
                   waits=[ldt, stok[par]], inc="dve")
        cast_prev[par] = tcast
        tsq = op("scalar", lambda e, o=sq, i=xb: e.activation(out=o, in_=i, func=AF.Square),
                 waits=[tcast], inc="act")
        first, last = fc == 0, fc == FC - 1
        for seg in range(4):
            bsum, csum = bank(seg)
            bsq_, csq = bank(4 + seg)
            op("tensor", lambda e, o=bsum[0:1, csum:csum + 512], l=ONESb[:, 0:1],
               r=xb[:, seg * 512:seg * 512 + 512], st=first, sp=last:
               e.matmul(o, l, r, start=st, stop=sp), waits=[tcast, tsq])
            ptok = op("tensor", lambda e, o=bsq_[0:1, csq:csq + 512], l=ONESb[:, 1:2],
                      r=sq[:, seg * 512:seg * 512 + 512], st=first, sp=last:
                      e.matmul(o, l, r, start=st, stop=sp), inc="pe")
        stok[par] = ptok
    pe_st = NOW("pe")
    pe_bc = None
    for seg in range(4):
        pe_bc = ln_seg_chain(seg, 4 + seg, pe_st, pe_bc)
    xn_prev = None
    for fc in range(FC):
        xtp = XT[:, (fc % 2) * TPC:(fc % 2) * TPC + TPC]
        ld2 = dma("sync", xtp, asrc[fc, :, 0:TPC], waits=[xn_prev])
        for seg in range(4):
            br, cr = bank(seg)
            bn, cn = bank(4 + seg)
            xts = xtp[:, seg * 512:seg * 512 + 512]
            scrf = HBf[:, seg * 512:seg * 512 + 512]
            op("vector", lambda e, o=scrf, a=xts, b=br[:, cr:cr + 512]:
               e.tensor_tensor(out=o, in0=a, in1=b, op=ALU.mult), waits=[ld2, pe_bc])
            t2 = op("vector", lambda e,
                    o=AR1b[:, fc * TPC + seg * 512:fc * TPC + seg * 512 + 512],
                    a=scrf, b=bn[:, cn:cn + 512]:
                    e.tensor_tensor(out=o, in0=a, in1=b, op=ALU.add), inc="dve")
        xn_prev = t2
    dve_xn2 = NOW("dve")
    XN = lambda kc: AR1b[:, kc * TPC:(kc + 1) * TPC]

    QB = HB[:, 0:2048]
    KB = HB[:, 2048:4096]
    KT = HB[:, 4096:5120]
    VB = HB[:, 5120:7168]
    VTBASE = 7168
    VT = lambda tt: HB[:, VTBASE + tt * 136:VTBASE + tt * 136 + 136]
    tones_v = op("vector", lambda e: e.memset(
        HB[:, VTBASE + 128:VTBASE + 16 * 136:136], 1.0), waits=[dve_xn2], inc="dve")

    kv_toks = []
    qs_done = {}
    bk_rd = {i: None for i in range(8)}
    t_kvbA = t_kvbB = None
    for h in range(NH):
        base = 64 * (h % 2)
        qcol = (h // 2) * 2048
        pq, tq = load_panel(wqT[h, :, :])
        pk, tk = load_panel(wkT[h, :, :])
        pv, tv_ = load_panel(wvT[h, :, :])
        qb_toks = []
        for ts in range(4):
            bsl = ts % 2
            bq_, cq_ = bank(bsl)
            wv = [tq, dve_xn2, bk_rd[bsl]]
            for kc in range(FC):
                st, sp = kc == 0, kc == FC - 1
                op("tensor", lambda e, o=bq_[:, cq_:cq_ + 512],
                   l=PAN[pq][:, kc * 128:(kc + 1) * 128],
                   r=XN(kc)[:, ts * 512:ts * 512 + 512], st=st, sp=sp:
                   e.matmul(o, l, r, start=st, stop=sp),
                   waits=wv if st else (), inc="pe" if sp else None)
            pe_mm = ("pe", cnt["pe"])
            mark_panel(pq, pe_mm)
            tev = op("vector", lambda e, o=QB[:, ts * 512:ts * 512 + 512],
                     i=bq_[:, cq_:cq_ + 512], b=BIAS[:, 0 * FC + h:0 * FC + h + 1]:
                     e.tensor_scalar(out=o, in0=i, scalar1=b, scalar2=None, op0=ALU.add),
                     waits=[pe_mm, CONST_DMS], inc="dve")
            bk_rd[bsl] = tev
            qb_toks.append(tev)
        qs_t = []
        for ts in range(4):
            bsl = 4 + ts % 2
            bqs, cqs = bank(bsl)
            tmm = op("tensor", lambda e, o=bqs[0:64, cqs:cqs + 512], l=WMB[:, 0:64],
                     r=QB[:, ts * 512:ts * 512 + 512]:
                     e.matmul(o, l, r, start=True, stop=True),
                     waits=[qb_toks[ts], CONST_DMG, bk_rd[bsl]], inc="pe")
            scr1 = XT[0:64, (ts % 2) * 512:(ts % 2) * 512 + 512]
            scr2 = XT[0:64, 1024 + (ts % 2) * 512:1024 + (ts % 2) * 512 + 512]
            t1 = op("vector", lambda e, o=scr1, i=bqs[0:64, cqs:cqs + 512]:
                    e.tensor_scalar_min(o, i, 0.0),
                    waits=[tmm], inc="dve")
            t2 = op("scalar", lambda e, o=scr2, i=scr1:
                    e.activation(out=o, in_=i, func=AF.Exp), waits=[t1], inc="act")
            t3 = op("vector", lambda e,
                    o=AR2[base:base + 64, qcol + ts * 512:qcol + ts * 512 + 512],
                    i=bqs[0:64, cqs:cqs + 512], x2=scr2:
                    e.scalar_tensor_tensor(out=o, in0=i, scalar=0.0, in1=x2,
                                           op0=ALU.max, op1=ALU.add),
                    waits=[t2], inc="dve")
            bk_rd[bsl] = t3
            qs_t.append(t3)
        qs_done[h] = qs_t[-1]
        kb_toks = []
        for ts in range(4):
            bsl = ts % 2
            bk_, ck_ = bank(bsl)
            wv = [tk, dve_xn2, bk_rd[bsl]]
            for kc in range(FC):
                st, sp = kc == 0, kc == FC - 1
                op("tensor", lambda e, o=bk_[:, ck_:ck_ + 512],
                   l=PAN[pk][:, kc * 128:(kc + 1) * 128],
                   r=XN(kc)[:, ts * 512:ts * 512 + 512], st=st, sp=sp:
                   e.matmul(o, l, r, start=st, stop=sp),
                   waits=wv if st else (), inc="pe" if sp else None)
            pe_mm = ("pe", cnt["pe"])
            mark_panel(pk, pe_mm)
            tev = op("vector", lambda e, o=KB[:, ts * 512:ts * 512 + 512],
                     i=bk_[:, ck_:ck_ + 512], b=BIAS[:, 1 * FC + h:1 * FC + h + 1]:
                     e.tensor_scalar(out=o, in0=i, scalar1=b, scalar2=None, op0=ALU.add),
                     waits=[pe_mm], inc="dve")
            bk_rd[bsl] = tev
            kb_toks.append(tev)
        ks_t = []
        for grp in range(2):
            bsl = 6 + grp
            bks, cks = bank(bsl)
            wv = kb_toks + [CONST_DMG, bk_rd[bsl]]
            for j in range(8):
                tt = grp * 8 + j
                st, sp = j == 0, j == 7
                op("tensor", lambda e, o=bks[:, cks + j * 64:cks + j * 64 + 64],
                   l=KB[:, tt * 128:tt * 128 + 128], r=WMB[:, 64:128], st=st, sp=sp:
                   e.matmul(o, l, r, start=st, stop=sp),
                   waits=wv if st else (), inc="pe" if sp else None)
            pe_mm = ("pe", cnt["pe"])
            scr1 = XT[:, grp * 512:grp * 512 + 512]
            scr2 = XT[:, 1024 + grp * 512:1024 + grp * 512 + 512]
            t1 = op("vector", lambda e, o=scr1, i=bks[:, cks:cks + 512]:
                    e.tensor_scalar_min(o, i, 0.0), waits=[pe_mm], inc="dve")
            t2 = op("scalar", lambda e, o=scr2, i=scr1:
                    e.activation(out=o, in_=i, func=AF.Exp), waits=[t1], inc="act")
            t3 = op("vector", lambda e, o=KT[:, grp * 512:grp * 512 + 512],
                    i=bks[:, cks:cks + 512], x2=scr2:
                    e.scalar_tensor_tensor(out=o, in0=i, scalar=0.0, in1=x2,
                                           op0=ALU.max, op1=ALU.add), waits=[t2], inc="dve")
            bk_rd[bsl] = t3
            ks_t.append(t3)
        vb_toks = []
        for ts in range(4):
            bsl = ts % 2
            bv_, cv_ = bank(bsl)
            wv = [tv_, dve_xn2, bk_rd[bsl]]
            for kc in range(FC):
                st, sp = kc == 0, kc == FC - 1
                op("tensor", lambda e, o=bv_[:, cv_:cv_ + 512],
                   l=PAN[pv][:, kc * 128:(kc + 1) * 128],
                   r=XN(kc)[:, ts * 512:ts * 512 + 512], st=st, sp=sp:
                   e.matmul(o, l, r, start=st, stop=sp),
                   waits=wv if st else (), inc="pe" if sp else None)
            pe_mm = ("pe", cnt["pe"])
            mark_panel(pv, pe_mm)
            tev = op("vector", lambda e, o=VB[:, ts * 512:ts * 512 + 512],
                     i=bv_[:, cv_:cv_ + 512], b=BIAS[:, 2 * FC + h:2 * FC + h + 1]:
                     e.tensor_scalar(out=o, in0=i, scalar1=b, scalar2=None, op0=ALU.add),
                     waits=[pe_mm], inc="dve")
            bk_rd[bsl] = tev
            vb_toks.append(tev)
        vt_toks = []
        vt4 = HB[:, VTBASE:VTBASE + 16 * 136].rearrange("p (t c) -> p t c", c=136)
        for tq in range(4):
            bslot = 2 + tq % 2
            bxb, cxb = bankb(bslot)
            tq_last = None
            for kk in range(4):
                tt = tq * 4 + kk
                wv = [vb_toks[tq], tones_v, bk_rd[bslot]] if kk == 0 else ()
                tq_last = op("tensor", lambda e, o=bxb[:, cxb + kk * 128:cxb + kk * 128 + 128],
                             i=VB[:, tt * 128:tt * 128 + 128]:
                             e.transpose(o, i, IDENTB[:, :]), waits=wv,
                             inc="pe" if kk == 3 else None)
            tev = op("vector", lambda e, o=vt4[:, tq * 4:tq * 4 + 4, 0:128],
                     i=bxb[:, cxb:cxb + 512].rearrange("p (t c) -> p t c", c=128):
                     e.tensor_copy(out=o, in_=i), waits=[tq_last], inc="dve")
            bk_rd[bslot] = tev
            vt_toks.append(tev)
        bsl = 4 + (h % 2)
        bkv, ckv = bank(bsl)
        wv = ks_t + vt_toks + [bk_rd[bsl]]
        for tt in range(16):
            st, sp = tt == 0, tt == 15
            op("tensor", lambda e, o=bkv[0:64, ckv:ckv + 129],
               l=KT[:, tt * 64:tt * 64 + 64], r=VT(tt)[:, 0:129], st=st, sp=sp:
               e.matmul(o, l, r, start=st, stop=sp),
               waits=wv if st else (), inc="pe" if sp else None)
        pe_kv = ("pe", cnt["pe"])
        tkv = op("vector", lambda e,
                 o=KVP[base:base + 64, (h // 2) * 132:(h // 2) * 132 + 129],
                 i=bkv[0:64, ckv:ckv + 129]: e.tensor_copy(out=o, in_=i),
                 waits=[pe_kv], inc="dve")
        bk_rd[bsl] = tkv
        kv_toks.append(tkv)
        if h == 7:
            t_ccwA = dma("sync", cc_inA[:, :], KVP[:, 0:528], waits=kv_toks[:8])
            t_ccA = op("gpsimd", lambda e: e.collective_compute(
                "AllReduce", ALU.add, replica_groups=[[0, 1], [2, 3], [4, 5], [6, 7]],
                ins=[cc_inA.opt()], outs=[cc_outA.opt()]), waits=[t_ccwA], inc="ccs", amt=1)
            t_kvbA = dma("gpsimd", KVB[:, 0:528], cc_outA[:, :], waits=[t_ccA])
        if h == 15:
            t_ccwB = dma("sync", cc_inB[:, :], KVP[:, 528:1056], waits=kv_toks[8:])
            t_ccB = op("gpsimd", lambda e: e.collective_compute(
                "AllReduce", ALU.add, replica_groups=[[0, 1], [2, 3], [4, 5], [6, 7]],
                ins=[cc_inB.opt()], outs=[cc_outB.opt()]), waits=[t_ccwB], inc="cc2", amt=1)
            t_kvbB = dma("gpsimd", KVB[:, 528:1056], cc_outB[:, :], waits=[t_ccB])
    DVE_HEADS = NOW("dve")
    bk_rd = {i: DVE_HEADS for i in range(8)}
    ao_toks = []
    nrow_war = None
    for h in range(NH):
        base = 64 * (h % 2)
        qcol = (h // 2) * 2048
        kvcol = (h // 2) * 132
        t_kvb = t_kvbA if h < 8 else t_kvbB
        QS = AR2[base:base + 64, qcol:qcol + 2048]
        tmms = []
        for ts in range(4):
            bn_, cn_ = bank(4 + ts)
            tmms.append(op("tensor", lambda e, o=bn_[0:1, cn_:cn_ + 512],
                           l=KVB[base:base + 64, kvcol + 128:kvcol + 129],
                           r=QS[:, ts * 512:ts * 512 + 512]:
                           e.matmul(o, l, r, start=True, stop=True),
                           waits=[t_kvb, qs_done[h], bk_rd[4 + ts]], inc="pe"))
        op("vector", lambda e: e.tensor_scalar(
            out=NROWS[0:1, 0:1024], in0=PS[2][0:1, 0:1024], scalar1=EPS_AT,
            scalar2=None, op0=ALU.add), waits=[tmms[0], tmms[1], nrow_war], inc="dve")
        op("vector", lambda e: e.tensor_scalar(
            out=NROWS[0:1, 1024:2048], in0=PS[3][0:1, 0:1024], scalar1=EPS_AT,
            scalar2=None, op0=ALU.add), waits=[tmms[2], tmms[3]], inc="dve")
        trc = op("vector", lambda e: e.reciprocal(out=NROWS[0:1, 0:2048],
                                                  in_=NROWS[0:1, 0:2048]), inc="dve")
        qd_t = []
        for ts in range(4):
            bn_, cn_ = bank(4 + ts)
            tmb = op("tensor", lambda e, o=bn_[0:64, cn_:cn_ + 512],
                     r=NROWS[0:1, ts * 512:ts * 512 + 512]:
                     e.matmul(o, ONESf[0:1, 0:64], r, start=True, stop=True),
                     waits=[trc], inc="pe")
            if ts == 3:
                nrow_war = tmb
            tqd = op("vector", lambda e, o=HB[base:base + 64, ts * 512:ts * 512 + 512],
                     a=QS[:, ts * 512:ts * 512 + 512], b=bn_[0:64, cn_:cn_ + 512]:
                     e.tensor_tensor(out=o, in0=a, in1=b, op=ALU.mult),
                     waits=[tmb], inc="dve")
            bk_rd[4 + ts] = tqd
            qd_t.append(tqd)
        for ts in range(4):
            bsl = ts % 2
            bo_, co_ = bank(bsl)
            tmm = op("tensor", lambda e, o=bo_[:, co_:co_ + 512],
                     l=KVB[base:base + 64, kvcol:kvcol + 128],
                     r=HB[base:base + 64, ts * 512:ts * 512 + 512]:
                     e.matmul(o, l, r, start=True, stop=True),
                     waits=[qd_t[ts], t_kvb, bk_rd[bsl]], inc="pe")
            tev = op("vector", lambda e,
                     o=AR1b[:, h * TPC + ts * 512:h * TPC + ts * 512 + 512],
                     i=bo_[:, co_:co_ + 512]: e.tensor_copy(out=o, in_=i),
                     waits=[tmm], inc="dve")
            bk_rd[bsl] = tev
            ao_toks.append(tev)
    AO = lambda hh: AR1b[:, hh * TPC:(hh + 1) * TPC]
    ldx_extra = [NOW("act"), NOW("dve")]
    st2 = {}
    p2dst = x_state
    for oc in range(FC):
        pw, tw = load_panel(woT[oc, :, :])
        xst = XT[:, (oc % 2) * TPC:(oc % 2) * TPC + TPC]
        wvx = list(ldx_extra) if oc < 2 else [st2[oc - 2]]
        ldx = dma("sync", xst, asrc[oc, :, 0:TPC], waits=wvx)
        for ts in range(4):
            bsl = 2 + ts % 2
            bw_, cw_ = bank(bsl)
            wv2 = [tw, bk_rd[bsl]] + (ao_toks if (oc == 0 and ts == 0) else [ao_toks[-1]])
            for hh in range(NH):
                st, sp = hh == 0, hh == NH - 1
                op("tensor", lambda e, o=bw_[:, cw_:cw_ + 512],
                   l=PAN[pw][:, hh * 128:(hh + 1) * 128],
                   r=AO(hh)[:, ts * 512:ts * 512 + 512], st=st, sp=sp:
                   e.matmul(o, l, r, start=st, stop=sp),
                   waits=wv2 if st else (), inc="pe" if sp else None)
            pe_mm = ("pe", cnt["pe"])
            mark_panel(pw, pe_mm)
            tev = op("vector", lambda e, o=xst[:, ts * 512:ts * 512 + 512],
                     i=bw_[:, cw_:cw_ + 512], b=BIAS[:, 3 * FC + oc:3 * FC + oc + 1]:
                     e.scalar_tensor_tensor(out=o, in0=i, scalar=b, in1=o,
                                            op0=ALU.add, op1=ALU.add),
                     waits=[pe_mm, ldx], inc="dve")
            bk_rd[bsl] = tev
        st2[oc] = dma("sync", p2dst[oc, :, 0:TPC], xst, waits=[bk_rd[2], bk_rd[3]])
    add_entry([NOW("dms")])

    add_entry([NOW("dml")])
    emit_ffn_sparse(x_state, y_out, CAP3, 1, echo_write=False)

    op("sync", lambda e: e.nop(), waits=[NOW("dms"), NOW("dmg"), NOW("dml")])

    with nc.Block() as block:
        def mk(engname):
            own = SELF_SER.get(engname)

            def body(eng):
                lw = {}
                done = 0
                for fn, waits, inc, amt in prog[engname]:
                    if own is not None and done > 0:
                        eng.wait_ge(sems[own], done)
                    for (s, v) in waits:
                        if lw.get(s, 0) < v:
                            eng.wait_ge(sems[s], v)
                            lw[s] = v
                    instr = fn(eng)
                    if inc is not None:
                        instr.then_inc(sems[inc], amt)
                        if inc == own:
                            done += amt
            return body
        block.sync(mk("sync"))
        block.scalar(mk("scalar"))
        block.vector(mk("vector"))
        block.tensor(mk("tensor"))
        block.gpsimd(mk("gpsimd"))
    ctx.close()
    return nc


def _prep_inputs(kw):
    f32 = np.float32
    x = np.asarray(kw["L0_attn_hidden_states"], f32).reshape(B * S, D)
    w1 = np.concatenate([np.asarray(kw["L0_ffn_sw1"], f32),
                         np.asarray(kw["L0_ffn_w1"], f32)], axis=0)
    w2 = np.concatenate([np.asarray(kw["L0_ffn_sw2"], f32),
                         np.asarray(kw["L0_ffn_w2"], f32)], axis=0)
    w1R = np.ascontiguousarray(
        w1.reshape(NEX, FC, 128, HC, 128).transpose(0, 3, 2, 1, 4)).reshape(NEX, HC, 128, FC * 128)
    w2R = np.ascontiguousarray(
        w2.reshape(NEX, HC, 128, FC, 128).transpose(0, 3, 2, 1, 4)).reshape(NEX, FC, 128, HC * 128)

    def wt(a):
        at = np.ascontiguousarray(np.asarray(a, f32).T)
        return np.ascontiguousarray(
            at.reshape(FC, 128, FC, 128).transpose(2, 1, 0, 3)).reshape(FC, 128, FC * 128)

    wq = wt(kw["L0_attn_wq"]); wk = wt(kw["L0_attn_wk"])
    wv = wt(kw["L0_attn_wv"]); wo = wt(kw["L0_attn_wo"])
    gate = np.ascontiguousarray(
        np.asarray(kw["L0_ffn_gate_w"], f32).T.reshape(FC, 128, NE).transpose(1, 0, 2)
    ).reshape(128, FC * NE)
    biasC = np.ascontiguousarray(
        np.stack([np.asarray(kw["L0_attn_bq"], f32), np.asarray(kw["L0_attn_bk"], f32),
                  np.asarray(kw["L0_attn_bv"], f32), np.asarray(kw["L0_attn_bo"], f32)]
                 ).reshape(4, FC, 128).transpose(2, 0, 1)).reshape(128, 4 * FC)
    wmapT = np.ascontiguousarray(
        np.concatenate([np.asarray(kw["L0_attn_wmap_q"], f32).T,
                        np.asarray(kw["L0_attn_wmap_k"], f32).T], axis=1))
    ident = np.eye(128, dtype=f32)

    expbase = np.stack([np.arange(16, dtype=f32) * CAP1,
                        np.arange(16, dtype=f32) * CAP3], axis=1)
    shared = dict(w1R=w1R, w2R=w2R, wqT=wq, wkT=wk, wvT=wv, woT=wo, gateT=gate,
                  biasC=biasC, wmapT=wmapT, ident_in=ident, expbase_in=expbase)
    in_maps = []
    for c in range(NCORES):
        xc = np.ascontiguousarray(x[c * TPC:(c + 1) * TPC, :].T).reshape(FC, 128, TPC)
        m = {"x0": xc}
        m.update(shared)
        in_maps.append(m)
    return in_maps


def kernel(**kw):
    in_maps = _prep_inputs(kw)
    nc = build_program()
    res = run_bass_kernel_spmd(nc, in_maps, core_ids=list(range(NCORES)))
    outs = []
    for c in range(NCORES):
        yc = np.asarray(res.results[c]["y"]).reshape(D, TPC)
        outs.append(np.ascontiguousarray(yc.T))
    return np.concatenate(outs, axis=0).reshape(B, S, D).astype(np.float32)


# version 09
# speedup vs baseline: 1.4578x; 1.0995x over previous
import numpy as np
from contextlib import ExitStack

import concourse.bass as bass
import concourse.mybir as mybir
from concourse.bass_utils import run_bass_kernel_spmd
from concourse.bass_core_rust import IndirectOffsetOnAxis

F32 = mybir.dt.float32
BF16 = mybir.dt.bfloat16
AF = mybir.ActivationFunctionType
ALU = mybir.AluOpType
AX = mybir.AxisListType

B, S, D = 4, 4096, 2048
NCORES = 8
TPC = (B * S) // NCORES
H_FF = 1408
NE = 16
NEX = NE + 1
NH, DH, DM = 16, 128, 64
FC = D // 128
HC = H_FF // 128
TG = 1024
NG = TPC // TG
EPS_LN = 1e-5
EPS_AT = 1e-6

ENGS = ("sync", "scalar", "vector", "tensor", "gpsimd")

CAP1 = 384
CAP3 = 512
CAPM = 512
USE_ECHO = False


def build_program():
    nc = bass.Bass("TRN2", target_bir_lowering=False, debug=False)

    x0 = nc.dram_tensor("x0", [FC, 128, TPC], F32, kind="ExternalInput").ap()
    w1R = nc.dram_tensor("w1R", [NEX, HC, 128, FC * 128], F32, kind="ExternalInput").ap()
    w2R = nc.dram_tensor("w2R", [NEX, FC, 128, HC * 128], F32, kind="ExternalInput").ap()
    wqT = nc.dram_tensor("wqT", [FC, 128, FC * 128], F32, kind="ExternalInput").ap()
    wkT = nc.dram_tensor("wkT", [FC, 128, FC * 128], F32, kind="ExternalInput").ap()
    wvT = nc.dram_tensor("wvT", [FC, 128, FC * 128], F32, kind="ExternalInput").ap()
    woT = nc.dram_tensor("woT", [FC, 128, FC * 128], F32, kind="ExternalInput").ap()
    gateT = nc.dram_tensor("gateT", [128, FC * NE], F32, kind="ExternalInput").ap()
    biasC = nc.dram_tensor("biasC", [128, 4 * FC], F32, kind="ExternalInput").ap()
    wmapT = nc.dram_tensor("wmapT", [128, 2 * DM], F32, kind="ExternalInput").ap()
    ident_in = nc.dram_tensor("ident_in", [128, 128], F32, kind="ExternalInput").ap()
    expbase_in = nc.dram_tensor("expbase_in", [16, 2], F32, kind="ExternalInput").ap()

    y_out = nc.dram_tensor("y", [FC, 128, TPC], F32, kind="ExternalOutput").ap()
    x_state = nc.dram_tensor("x_state", [FC, 128, TPC], F32, kind="Internal").ap()
    cc_inA = nc.dram_tensor("cc_inA", [128, 528], F32, kind="Internal").ap()
    cc_outA = nc.dram_tensor("cc_outA", [128, 528], F32, kind="Internal").ap()
    cc_inB = nc.dram_tensor("cc_inB", [128, 528], F32, kind="Internal").ap()
    cc_outB = nc.dram_tensor("cc_outB", [128, 528], F32, kind="Internal").ap()
    exp_xn = nc.dram_tensor("exp_xn", [NE * CAPM, D], BF16, kind="Internal").ap()
    y_dram = nc.dram_tensor("y_dram", [NE * CAPM, D], F32, kind="Internal").ap()
    w1E = nc.dram_tensor("w1E", [NEX, HC, 128, FC * 128], BF16, kind="Internal").ap()
    w2E = nc.dram_tensor("w2E", [NEX, FC, 128, HC * 128], BF16, kind="Internal").ap()

    ctx = ExitStack()
    sb = lambda name, shape, dt: ctx.enter_context(nc.sbuf_tensor(name, shape, dt))

    AR1 = sb("ar1", [128, FC * TG], F32)
    AR1b = AR1.bitcast(BF16)
    AR2 = sb("ar2", [128, FC * TG], BF16)
    AR2f = AR2.bitcast(F32)
    HB = sb("hb", [128, 9728], BF16)
    HBf = HB.bitcast(F32)
    SHH = sb("shh", [128, HC * 512], BF16)
    hstage = lambda jc, ts: (SHH[:, jc * 512:jc * 512 + 512] if ts == 0
                             else HB[:, 4096 + jc * 512:4096 + jc * 512 + 512])
    PAN = [sb(f"pan{i}", [128, FC * 128], BF16) for i in range(4)]
    XT = sb("xts", [128, 2 * TPC], F32)
    S1 = sb("s1", [128, 2 * TPC], BF16)
    ROWS = sb("rows", [1, 5 * 512], F32)
    R_MU = ROWS[0:1, 0:512]
    R_VAR = ROWS[0:1, 512:1024]
    R_RR = ROWS[0:1, 1024:1536]
    R_NMR = ROWS[0:1, 1536:2048]
    R_SCR = ROWS[0:1, 2048:2560]
    NROWS = ROWS
    RT = sb("rt", [128, 128], F32)
    CMB = sb("cmb", [16, TG], F32)
    BIAS = sb("biasb", [128, 4 * FC], F32)
    GTE = sb("gte", [128, FC * NE], BF16)
    WMB = sb("wmb", [128, 2 * DM], BF16)
    IDENT = sb("identb", [128, 128], F32)
    ONESb = sb("onesb", [128, 4], BF16)
    ONESf = sb("onesf", [1, 128], F32)
    KVP = sb("kvp", [128, 8 * 132], F32)
    KVB = sb("kvb", [128, 8 * 132], BF16)
    IND0B = sb("ind0b", [128, 256], F32)
    INDAB = sb("indab", [128, 256], F32)
    CWTB = sb("cwtb", [128, 256], F32)
    RTS = sb("rts", [128, 64], F32)
    DSTIB = sb("dstib", [128, 32], mybir.dt.int32)
    CRB = sb("crb", [128, 32], F32)
    CNTB = sb("cntb", [16, 2], F32)
    EXPB = sb("expb", [16, 2], F32)
    IDENTB = sb("identbb", [128, 128], BF16)
    S1f = S1.bitcast(F32)
    XTb = XT.bitcast(BF16)

    PS = [ctx.enter_context(nc.psum_tensor(f"psb{i}", [128, 1024], F32)) for i in range(4)]
    PSb = [p.bitcast(BF16) for p in PS]

    def bank(i):
        return PS[i // 2], (i % 2) * 512

    def bankb(i):
        return PSb[i // 2], (i % 2) * 1024

    sems = {n: ctx.enter_context(nc.semaphore(n)) for n in
            ("dms", "dmg", "dml", "pe", "act", "dve", "ccs", "cc2")}
    cnt = {n: 0 for n in sems}
    prog = {e: [] for e in ENGS}
    OWNER = {"pe": "tensor", "act": "scalar", "dve": "vector"}

    pending = {e: [] for e in ENGS}

    def add_entry(toks):
        for e in ENGS:
            pending[e].extend(t for t in toks if t is not None)

    SELF_SER = {"vector": "dve", "scalar": "act"}

    def op(eng, fn, waits=(), inc=None, amt=1):
        w = [t for t in waits if t is not None]
        if pending[eng]:
            w = pending[eng] + w
            pending[eng] = []
        w = tuple((s, v) for (s, v) in w if v > 0 and OWNER.get(s) != eng)
        if inc is None and eng in SELF_SER:
            inc = SELF_SER[eng]
        tok = None
        if inc is not None:
            cnt[inc] += amt
            tok = (inc, cnt[inc])
        prog[eng].append((fn, w, inc, amt))
        return tok

    def dma(queue, out_ap, in_ap, waits=()):
        q = {"sync": "dms", "gpsimd": "dmg", "scalar": "dml"}[queue]
        return op(queue, lambda e, o=out_ap, i=in_ap: e.dma_start(out=o, in_=i),
                  waits=waits, inc=q, amt=16)

    NOW = lambda s: (s, cnt[s])

    dma("sync", BIAS[:, :], biasC[:, :])
    dma("sync", IDENT[:, :], ident_in[:, :])
    dma("gpsimd", GTE[:, :], gateT[:, :])
    dma("gpsimd", WMB[:, :], wmapT[:, :])
    dma("sync", EXPB[:, :], expbase_in[:, :])
    dma("gpsimd", IDENTB[:, :], ident_in[:, :])
    op("vector", lambda e: e.memset(ONESb[:, :], 1.0), inc="dve")
    op("vector", lambda e: e.memset(ONESf[:, :], 1.0), inc="dve")
    CONST_DMS = NOW("dms")
    CONST_DMG = NOW("dmg")
    CONST_DVE = NOW("dve")

    pan_state = [{"r": None, "e": None} for _ in range(4)]
    pan_ptr = {"i": 0}

    def load_panel(src_ap, echo_to=None):
        i = pan_ptr["i"] % 4
        pan_ptr["i"] += 1
        tok = dma("gpsimd", PAN[i][:, 0:src_ap.shape[-1]], src_ap,
                  waits=[pan_state[i]["r"], pan_state[i]["e"]])
        if echo_to is not None:
            pan_state[i]["e"] = dma("scalar", echo_to,
                                    PAN[i][:, 0:src_ap.shape[-1]], waits=[tok])
        return i, tok

    def mark_panel(i, pe_tok):
        pan_state[i]["r"] = pe_tok

    def ln_seg_chain(bsum_i, bsq_i, pe_stats, war_pe):
        bsum, csum = bank(bsum_i)
        bsq_, csq = bank(bsq_i)
        op("vector", lambda e: e.tensor_scalar(out=R_MU, in0=bsum[0:1, csum:csum + 512],
                                               scalar1=1.0 / D, scalar2=None, op0=ALU.mult),
           waits=[pe_stats, war_pe])
        op("vector", lambda e: e.tensor_scalar(out=R_VAR, in0=bsq_[0:1, csq:csq + 512],
                                               scalar1=1.0 / D, scalar2=EPS_LN,
                                               op0=ALU.mult, op1=ALU.add))
        op("vector", lambda e: e.tensor_tensor(out=R_SCR, in0=R_MU, in1=R_MU, op=ALU.mult))
        tvar = op("vector", lambda e: e.tensor_tensor(out=R_VAR, in0=R_VAR, in1=R_SCR,
                                                      op=ALU.subtract), inc="dve")
        tstd = op("scalar", lambda e: e.activation(out=R_VAR, in_=R_VAR, func=AF.Sqrt),
                  waits=[tvar], inc="act")
        trec = op("vector", lambda e: e.reciprocal(out=R_RR, in_=R_VAR), waits=[tstd], inc="dve")
        tnmr = op("vector", lambda e: e.scalar_tensor_tensor(
            out=R_NMR, in0=R_MU, scalar=-1.0, in1=R_RR, op0=ALU.mult, op1=ALU.mult), inc="dve")
        op("tensor", lambda e: e.matmul(bsum[:, csum:csum + 512], ONESf[0:1, :], R_RR,
                                        start=True, stop=True), waits=[trec, CONST_DVE])
        tok = op("tensor", lambda e: e.matmul(bsq_[:, csq:csq + 512], ONESf[0:1, :], R_NMR,
                                              start=True, stop=True), waits=[tnmr], inc="pe")
        return tok

    pairs = [(0, 1), (0, 2), (0, 3), (1, 2), (1, 3), (2, 3)]

    def emit_ffn_sparse(src, dst, cap, expcol, echo_write):
        IOA = IndirectOffsetOnAxis
        ci = cap // 128

        fs = {
            "pe_stats_par": {0: None, 1: None},
            "stats_war": NOW("dve"),
            "scat_toks": [],
            "bank_rd": {i: None for i in range(8)},
            "t_cnt": None,
            "prev_tev": None,
        }
        bank_rd = fs["bank_rd"]

        def emit_front_ln(g):
            t0 = g * TG
            ld = []
            for fc in range(FC):
                ld.append(dma("sync", AR1[:, fc * TG:(fc + 1) * TG],
                              src[fc, :, t0:t0 + TG]))
            for fc in range(FC):
                xg = AR1[:, fc * TG:(fc + 1) * TG]
                xb = S1[:, (fc % 2) * TG:(fc % 2) * TG + TG]
                sq = S1[:, 2 * TG + (fc % 2) * TG:2 * TG + (fc % 2) * TG + TG]
                tcast = op("vector", lambda e, o=xb, i=xg: e.tensor_copy(out=o, in_=i),
                           waits=[ld[fc], fs["pe_stats_par"][fc % 2]])
                tsq = op("scalar", lambda e, o=sq, i=xb: e.activation(out=o, in_=i, func=AF.Square),
                         waits=[tcast])
                first, last = fc == 0, fc == FC - 1
                for seg in range(2):
                    bsum, csum = bank(4 + 2 * seg)
                    bsq_, csq = bank(5 + 2 * seg)
                    wmm = [tcast, tsq] + ([fs["stats_war"]] if first else [])
                    op("tensor", lambda e, o=bsum[0:1, csum:csum + 512], l=ONESb[:, 0:1],
                       r=xb[:, seg * 512:seg * 512 + 512], st=first, sp=last:
                       e.matmul(o, l, r, start=st, stop=sp), waits=wmm if first else [tcast, tsq])
                    ptok = op("tensor", lambda e, o=bsq_[0:1, csq:csq + 512], l=ONESb[:, 1:2],
                              r=sq[:, seg * 512:seg * 512 + 512], st=first, sp=last:
                              e.matmul(o, l, r, start=st, stop=sp), inc="pe")
                fs["pe_stats_par"][fc % 2] = ptok
            pe_stats = NOW("pe")
            pe_bc = None
            for seg in range(2):
                pe_bc = ln_seg_chain(4 + 2 * seg, 5 + 2 * seg, pe_stats, pe_bc)
            for fc in range(FC):
                for seg in range(2):
                    br, cr = bank(4 + 2 * seg)
                    bn, cn = bank(5 + 2 * seg)
                    xgs = AR1[:, fc * TG + seg * 512:fc * TG + seg * 512 + 512]
                    tts = XT[:, (fc % 4) * 512:(fc % 4) * 512 + 512]
                    xns = AR2[:, fc * TG + seg * 512:fc * TG + seg * 512 + 512]
                    op("vector", lambda e, o=tts, a=xgs, b=br[:, cr:cr + 512]:
                       e.tensor_tensor(out=o, in0=a, in1=b, op=ALU.mult), waits=[pe_bc])
                    op("vector", lambda e, o=xns, a=tts, b=bn[:, cn:cn + 512]:
                       e.tensor_tensor(out=o, in0=a, in1=b, op=ALU.add))
            return NOW("dve"), pe_stats

        def routing_scatter_gen(g, dve_xn, pe_stats):
            gsl = slice(g * 128, (g + 1) * 128)
            rt_toks = []
            bs, cs = bank(6)
            for tt in range(8):
                for kc in range(FC):
                    st, sp = kc == 0, kc == FC - 1
                    op("tensor", lambda e, o=bs[0:128, cs + tt * 16:cs + tt * 16 + 16],
                       l=AR2[:, kc * TG + tt * 128:kc * TG + tt * 128 + 128],
                       r=GTE[:, kc * NE:(kc + 1) * NE], st=st, sp=sp:
                       e.matmul(o, l, r, start=st, stop=sp),
                       waits=[dve_xn, CONST_DMG, fs.get("sig_prev")] if (st and tt == 0)
                       else ([dve_xn] if st else ()),
                       inc="pe" if (sp and tt == 7) else None)
                if tt % 2 == 1:
                    yield
            pe_gate = ("pe", cnt["pe"])
            sig = op("scalar", lambda e: e.activation(
                out=RT[:, 0:128], in_=bs[0:128, cs:cs + 128], func=AF.Sigmoid),
                waits=[pe_gate])
            fs["sig_prev"] = sig
            ps = lambda pi: XT[:, pi * 32:(pi + 1) * 32]
            gsb = XT[:, 192:224]
            pmin = lambda pi: XT[:, 224 + pi * 8:232 + pi * 8]
            thr = XT[:, 272:280]
            sel = XT[:, 288:320]
            mbuf = XT[:, 320:448]
            mx1 = XT[:, 448:456]
            mx2 = XT[:, 456:464]
            den = XT[:, 464:472]
            m2 = XT[:, 512:640]
            tmask = XT[:, 640:768]
            r3 = lambda ap: ap.rearrange("p (t e) -> p t e", e=16)
            bc = lambda ap: ap.unsqueeze(-1).broadcast_to((128, 8, 16))
            for pi, (a, b) in enumerate(pairs):
                op("vector", lambda e, a_=a, b_=b, pi_=pi: e.tensor_tensor(
                    out=ps(pi_), in0=RT[:, a_:128:4], in1=RT[:, b_:128:4], op=ALU.add),
                    waits=[sig] if pi == 0 else ())
            op("vector", lambda e: e.tensor_tensor(out=gsb, in0=ps(0), in1=ps(1), op=ALU.max))
            for pi in range(2, 6):
                op("vector", lambda e, pi_=pi: e.tensor_tensor(
                    out=gsb, in0=gsb, in1=ps(pi_), op=ALU.max))
            for pi, (a, b) in enumerate(pairs):
                op("vector", lambda e, a_=a, b_=b, pi_=pi: e.tensor_tensor(
                    out=pmin(pi_), in0=XT[:, 192 + a_:224:4], in1=XT[:, 192 + b_:224:4],
                    op=ALU.min))
            op("vector", lambda e: e.tensor_tensor(out=thr, in0=pmin(0), in1=pmin(1), op=ALU.max))
            for pi in range(2, 6):
                op("vector", lambda e, pi_=pi: e.tensor_tensor(
                    out=thr, in0=thr, in1=pmin(pi_), op=ALU.max))
            for g4 in range(4):
                op("vector", lambda e, g_=g4: e.tensor_tensor(
                    out=XT[:, 288 + g_:320:4], in0=XT[:, 192 + g_:224:4], in1=thr,
                    op=ALU.is_ge))
            for a in range(4):
                op("vector", lambda e, a_=a: e.tensor_tensor(
                    out=XT[:, 320 + a_:448:4], in0=RT[:, a_:128:4], in1=sel, op=ALU.mult))
            op("vector", lambda e: e.tensor_reduce(out=mx1.unsqueeze(-1), in_=r3(mbuf),
                                                   op=ALU.max, axis=AX.X))
            op("vector", lambda e: e.tensor_tensor(out=r3(IND0B[:, gsl]), in0=r3(mbuf),
                                                   in1=bc(mx1), op=ALU.is_ge))
            op("vector", lambda e: e.scalar_tensor_tensor(
                out=m2, in0=IND0B[:, gsl], scalar=-1e30, in1=mbuf,
                op0=ALU.mult, op1=ALU.add))
            op("vector", lambda e: e.tensor_reduce(out=mx2.unsqueeze(-1), in_=r3(m2),
                                                   op=ALU.max, axis=AX.X))
            op("vector", lambda e: e.tensor_tensor(out=r3(INDAB[:, gsl]), in0=r3(mbuf),
                                                   in1=bc(mx2), op=ALU.is_ge))
            op("vector", lambda e: e.tensor_tensor(out=den, in0=mx1, in1=mx2, op=ALU.add))
            op("vector", lambda e: e.tensor_scalar(out=den, in0=den, scalar1=1e-20,
                                                   scalar2=None, op0=ALU.add))
            op("vector", lambda e: e.reciprocal(out=den, in_=den))
            op("vector", lambda e: e.tensor_tensor(out=tmask, in0=mbuf, in1=INDAB[:, gsl],
                                                   op=ALU.mult))
            tcw_all = op("vector", lambda e: e.tensor_tensor(
                out=r3(CWTB[:, gsl]), in0=r3(tmask), in1=bc(den), op=ALU.mult))
            yield
            for tt in range(8):
                bt, ct = bank(7)
                ttr = op("tensor", lambda e, o=bt[0:16, ct:ct + 128],
                         i=CWTB[:, g * 128 + tt * 16:g * 128 + tt * 16 + 16]:
                         e.transpose(o, i, IDENT[:, :]),
                         waits=[tcw_all, CONST_DMS, fs["prev_tev"]], inc="pe")
                tev = op("vector", lambda e, o=CMB[0:16, tt * 128:tt * 128 + 128],
                         i=bt[0:16, ct:ct + 128]: e.tensor_copy(out=o, in_=i),
                         waits=[ttr])
                fs["prev_tev"] = tev
                rt_toks.append(tev)
                if tt % 2 == 1:
                    yield
            selv = XT[0:16, 2048:3072]
            dstf = XT[0:16, 3072:4096]
            cur = S1f[0:16, 0:1024]
            oth = S1f[0:16, 1024:2048]
            op("vector", lambda e: e.tensor_scalar(out=selv, in0=CMB[0:16, :], scalar1=0.0,
                                                   scalar2=None, op0=ALU.is_gt),
               waits=rt_toks + [pe_stats])
            op("vector", lambda e: e.tensor_copy(out=cur, in_=selv))
            sh = 1
            while sh < 1024:
                op("vector", lambda e, a=oth[0:16, 0:sh], b=cur[0:16, 0:sh]:
                   e.tensor_copy(out=a, in_=b))
                op("vector", lambda e, o=oth[0:16, sh:1024], a=cur[0:16, sh:1024],
                   b=cur[0:16, 0:1024 - sh]:
                   e.tensor_tensor(out=o, in0=a, in1=b, op=ALU.add))
                cur, oth = oth, cur
                sh *= 2
            op("vector", lambda e, c=cur: e.tensor_tensor(out=dstf, in0=c, in1=selv,
                                                          op=ALU.subtract))
            if g == 0:
                fs["t_cnt"] = op("vector", lambda e, c=cur: e.tensor_scalar(
                    out=CNTB[0:16, 0:1], in0=c[0:16, 1023:1024],
                    scalar1=float(cap - 1), scalar2=None, op0=ALU.min))
            else:
                op("vector", lambda e: e.tensor_scalar(out=dstf, in0=dstf,
                                                       scalar1=CNTB[0:16, 0:1], scalar2=None,
                                                       op0=ALU.add), waits=[fs["t_cnt"]])
            op("vector", lambda e: e.tensor_scalar(out=dstf, in0=dstf,
                                                   scalar1=float(cap - 1), scalar2=None,
                                                   op0=ALU.min))
            t_dstf = op("vector", lambda e: e.tensor_scalar(
                out=dstf, in0=dstf, scalar1=EXPB[0:16, expcol:expcol + 1], scalar2=None,
                op0=ALU.add), waits=[CONST_DMS])
            yield
            bt, ct = bank(7)
            tr2_toks = []
            for tt in range(8):
                tr2_toks.append(op("tensor", lambda e, o=bt[0:128, ct + tt * 16:ct + tt * 16 + 16],
                                   i=dstf[0:16, tt * 128:tt * 128 + 128]:
                                   e.transpose(o, i, IDENT[0:16, 0:16]),
                                   waits=[t_dstf, bank_rd[7], fs["prev_tev"]] if tt == 0 else (),
                                   inc="pe" if tt == 7 else None))
            dst_tm = XT[:, 768:896]
            rank1 = XT[:, 896:1024]
            prod = XT[:, 1024:1152]
            slot0 = XT[:, 1280:1288]
            slot1 = XT[:, 1288:1296]
            cw0 = XT[:, 1296:1304]
            cw1 = XT[:, 1304:1312]
            r3b = lambda ap: ap.rearrange("p (t e) -> p t e", e=16)
            tdstt = op("vector", lambda e: e.tensor_copy(out=dst_tm, in_=bt[0:128, ct:ct + 128]),
                       waits=[tr2_toks[-1]])
            bank_rd[7] = tdstt
            op("vector", lambda e: e.tensor_tensor(
                out=rank1, in0=INDAB[:, gsl], in1=IND0B[:, gsl], op=ALU.subtract))
            op("vector", lambda e: e.tensor_tensor(out=prod, in0=dst_tm, in1=IND0B[:, gsl],
                                                   op=ALU.mult))
            op("vector", lambda e: e.tensor_reduce(out=slot0.unsqueeze(-1), in_=r3b(prod),
                                                   op=ALU.add, axis=AX.X))
            op("vector", lambda e: e.tensor_tensor(out=prod, in0=dst_tm, in1=rank1,
                                                   op=ALU.mult))
            op("vector", lambda e: e.tensor_reduce(out=slot1.unsqueeze(-1), in_=r3b(prod),
                                                   op=ALU.add, axis=AX.X))
            op("vector", lambda e: e.tensor_tensor(out=prod, in0=CWTB[:, gsl],
                                                   in1=IND0B[:, gsl], op=ALU.mult))
            op("vector", lambda e: e.tensor_reduce(out=cw0.unsqueeze(-1), in_=r3b(prod),
                                                   op=ALU.add, axis=AX.X))
            op("vector", lambda e: e.tensor_tensor(out=prod, in0=CWTB[:, gsl],
                                                   in1=rank1, op=ALU.mult))
            op("vector", lambda e: e.tensor_reduce(out=cw1.unsqueeze(-1), in_=r3b(prod),
                                                   op=ALU.add, axis=AX.X))
            c0g = g * 16
            op("vector", lambda e: e.tensor_copy(out=DSTIB[:, c0g:c0g + 16:2], in_=slot0))
            op("vector", lambda e: e.tensor_copy(out=DSTIB[:, c0g + 1:c0g + 16:2], in_=slot1))
            op("vector", lambda e: e.tensor_copy(out=CRB[:, c0g:c0g + 16:2], in_=cw0))
            op("vector", lambda e: e.tensor_copy(out=CRB[:, c0g + 1:c0g + 16:2], in_=cw1))
            t_rank = NOW("dve")
            yield
            for tt in range(TG // 128):
                gi = g * 8 + tt
                xbuf = HB[:, (tt % 2) * 2048:(tt % 2) * 2048 + 2048]
                ev_toks = []
                for kq in range(4):
                    bslot = 4 + kq % 2
                    bxb, cxb = bankb(bslot)
                    tq_toks = []
                    for kk in range(4):
                        kc = kq * 4 + kk
                        w = [dve_xn, CONST_DMG, bank_rd[bslot]] if kk == 0 else ()
                        tq_toks.append(op("tensor", lambda e,
                                          o=bxb[:, cxb + kk * 128:cxb + kk * 128 + 128],
                                          i=AR2[:, kc * TG + tt * 128:kc * TG + tt * 128 + 128]:
                                          e.transpose(o, i, IDENTB[:, :]), waits=w,
                                          inc="pe" if kk == 3 else None))
                    wv2 = [tq_toks[-1]]
                    if kq == 0 and len(fs["scat_toks"]) >= 2:
                        wv2.append(fs["scat_toks"][-2])
                    tev2 = op("vector", lambda e, o=xbuf[:, kq * 512:(kq + 1) * 512],
                              i=bxb[:, cxb:cxb + 512]: e.tensor_copy(out=o, in_=i),
                              waits=wv2)
                    bank_rd[bslot] = tev2
                    ev_toks.append(tev2)
                for r in range(2):
                    fs["scat_toks"].append(op("gpsimd", lambda e, c=gi * 2 + r, xb_=xbuf:
                                              e.indirect_dma_start(
                                                  exp_xn[:, :], IOA(DSTIB[:, c:c + 1], 0),
                                                  xb_[:, :], None),
                                              waits=[t_rank, ev_toks[-1]], inc="dmg", amt=16))
                yield

        def shared_gen(g, dve_xn, sh_toks):
            if not USE_ECHO:
                ld_w1 = lambda jc: load_panel(w1R[0, jc, :, :])
                ld_w2 = lambda dc: load_panel(w2R[0, dc, :, :])
            elif echo_write and g == 0:
                ld_w1 = lambda jc: load_panel(w1R[0, jc, :, :], echo_to=w1E[0, jc, :, :])
                ld_w2 = lambda dc: load_panel(w2R[0, dc, :, :], echo_to=w2E[0, dc, :, :])
            else:
                pending["gpsimd"].append(("dml", cnt["dml"]))
                ld_w1 = lambda jc: load_panel(w1E[0, jc, :, :])
                ld_w2 = lambda dc: load_panel(w2E[0, dc, :, :])
            for jc in range(HC):
                p_i, p_tok = ld_w1(jc)
                for ts in range(2):
                    bsl = (2 * jc + ts) % 2
                    bh, ch = bank(bsl)
                    wvm = [p_tok, dve_xn, bank_rd[bsl]]
                    for kc in range(FC):
                        st, sp = kc == 0, kc == FC - 1
                        op("tensor", lambda e, o=bh[:, ch:ch + 512],
                           l=PAN[p_i][:, kc * 128:(kc + 1) * 128],
                           r=AR2[:, kc * TG + ts * 512:kc * TG + ts * 512 + 512],
                           st=st, sp=sp: e.matmul(o, l, r, start=st, stop=sp),
                           waits=wvm if st else (), inc="pe" if sp else None)
                    pe_mm = ("pe", cnt["pe"])
                    mark_panel(p_i, pe_mm)
                    tact = op("scalar", lambda e, o=hstage(jc, ts),
                              i=bh[:, ch:ch + 512]: e.activation(out=o, in_=i, func=AF.Silu),
                              waits=[pe_mm])
                    bank_rd[bsl] = tact
                yield
            silu_last = tact
            for dc in range(FC):
                p_i, p_tok = ld_w2(dc)
                for ts in range(2):
                    bsl = 2 + (2 * dc + ts) % 2
                    by, cy = bank(bsl)
                    wvm = [p_tok, silu_last, bank_rd[bsl]]
                    for hc in range(HC):
                        st, sp = hc == 0, hc == HC - 1
                        op("tensor", lambda e, o=by[:, cy:cy + 512],
                           l=PAN[p_i][:, hc * 128:(hc + 1) * 128],
                           r=hstage(hc, ts), st=st, sp=sp:
                           e.matmul(o, l, r, start=st, stop=sp),
                           waits=wvm if st else (), inc="pe" if sp else None)
                    pe_mm = ("pe", cnt["pe"])
                    mark_panel(p_i, pe_mm)
                    xgs = AR1[:, dc * TG + ts * 512:dc * TG + ts * 512 + 512]
                    tacc = op("vector", lambda e, o=xgs, a=by[:, cy:cy + 512]:
                              e.scalar_tensor_tensor(out=o, in0=a, scalar=0.5, in1=o,
                                                     op0=ALU.mult, op1=ALU.add),
                              waits=[pe_mm, sh_toks[dc]])
                    bank_rd[bsl] = tacc
                    sh_toks[dc] = tacc
                yield

        for g in range(NG):
            t0 = g * TG
            dve_xn, pe_stats = emit_front_ln(g)
            sh_toks = {fc: None for fc in range(FC)}
            rg = routing_scatter_gen(g, dve_xn, pe_stats)
            sg = shared_gen(g, dve_xn, sh_toks)
            r_alive = s_alive = True
            while r_alive or s_alive:
                if r_alive:
                    try:
                        next(rg)
                    except StopIteration:
                        r_alive = False
                if s_alive:
                    try:
                        next(sg)
                    except StopIteration:
                        s_alive = False
            for fc in range(FC):
                dma("sync", dst[fc, :, t0:t0 + TG], AR1[:, fc * TG:(fc + 1) * TG],
                    waits=[sh_toks[fc]])
            add_entry([NOW("dms")])

        AR1f = AR1
        yr0 = (8192 + HC * cap + 1) // 2
        yr0 = (yr0 + 127) & ~127
        YALL = lambda i: AR1f[:, yr0 + i * 2048:yr0 + (i + 1) * 2048]
        assert yr0 + ci * 2048 <= FC * TG
        intr_pe = {}
        m1_done = None
        ystore_done = {}
        yrow_last = {}
        for ei in range(NE):
            eb = ei * cap
            XET = XTb[:, 0:ci * 2048] if ei % 2 == 0 else HB[:, 0:ci * 2048]
            wv = [fs["scat_toks"][-1]] if ei < 2 else [intr_pe[ei - 2]]
            if ei < 2:
                wv.append(NOW("dve"))
            t_ld = dma("gpsimd", XET.rearrange("p (i c) -> p i c", c=D),
                       exp_xn[eb:eb + cap, :].rearrange("(i p) c -> p i c", p=128),
                       waits=wv)
            first_evac = True
            ar1k = AR1b[:, 0:FC * cap].rearrange("p (k c) -> p k c", c=cap)
            for i in range(ci):
                for kq in range(4):
                    bslot = 4 + kq % 2
                    bxb, cxb = bankb(bslot)
                    tq_last = None
                    for kk in range(4):
                        kc = kq * 4 + kk
                        w = [t_ld, bank_rd[bslot]] if kk == 0 else ()
                        tq_last = op("tensor", lambda e,
                                     o=bxb[:, cxb + kk * 128:cxb + kk * 128 + 128],
                                     i=XET[:, i * 2048 + kc * 128:i * 2048 + kc * 128 + 128]:
                                     e.transpose(o, i, IDENTB[:, :]), waits=w,
                                     inc="pe" if kk == 3 else None)
                    wev = [tq_last]
                    if first_evac:
                        wev.append(m1_done)
                        if ei == 0:
                            wev.append(NOW("dms"))
                        first_evac = False
                    tev2 = op("vector", lambda e,
                              o=ar1k[:, kq * 4:kq * 4 + 4, i * 128:i * 128 + 128],
                              i=bxb[:, cxb:cxb + 512].rearrange("p (k c) -> p k c", c=128):
                              e.tensor_copy(out=o, in_=i), waits=wev)
                    bank_rd[bslot] = tev2
            intr_pe[ei] = ("pe", cnt["pe"])
            t_intr = NOW("dve")
            silu_last = None
            for jc in range(HC):
                if not USE_ECHO:
                    p_i, p_tok = load_panel(w1R[ei + 1, jc, :, :])
                elif echo_write:
                    p_i, p_tok = load_panel(w1R[ei + 1, jc, :, :],
                                            echo_to=w1E[ei + 1, jc, :, :])
                else:
                    p_i, p_tok = load_panel(w1E[ei + 1, jc, :, :])
                bsl = jc % 2
                bh, ch = bank(bsl)
                wvm = [p_tok, t_intr, bank_rd[bsl]]
                for kc in range(FC):
                    st, sp = kc == 0, kc == FC - 1
                    op("tensor", lambda e, o=bh[:, ch:ch + cap],
                       l=PAN[p_i][:, kc * 128:(kc + 1) * 128],
                       r=AR1b[:, kc * cap:kc * cap + cap], st=st, sp=sp:
                       e.matmul(o, l, r, start=st, stop=sp),
                       waits=wvm if st else (), inc="pe" if sp else None)
                pe_mm = ("pe", cnt["pe"])
                mark_panel(p_i, pe_mm)
                tact = op("scalar", lambda e, o=AR1b[:, 8192 + jc * cap:8192 + jc * cap + cap],
                          i=bh[:, ch:ch + cap]: e.activation(out=o, in_=i, func=AF.Silu),
                          waits=[pe_mm])
                bank_rd[bsl] = tact
                silu_last = tact
            m1_done = ("pe", cnt["pe"])
            for dc in range(FC):
                if not USE_ECHO:
                    p_i, p_tok = load_panel(w2R[ei + 1, dc, :, :])
                elif echo_write:
                    p_i, p_tok = load_panel(w2R[ei + 1, dc, :, :],
                                            echo_to=w2E[ei + 1, dc, :, :])
                else:
                    p_i, p_tok = load_panel(w2E[ei + 1, dc, :, :])
                bsl = 2 + dc % 2
                by, cy = bank(bsl)
                wvm = [p_tok, silu_last, bank_rd[bsl]]
                for hc in range(HC):
                    st, sp = hc == 0, hc == HC - 1
                    op("tensor", lambda e, o=by[:, cy:cy + cap],
                       l=PAN[p_i][:, hc * 128:(hc + 1) * 128],
                       r=AR1b[:, 8192 + hc * cap:8192 + hc * cap + cap], st=st, sp=sp:
                       e.matmul(o, l, r, start=st, stop=sp),
                       waits=wvm if st else (), inc="pe" if sp else None)
                pe_mm = ("pe", cnt["pe"])
                mark_panel(p_i, pe_mm)
                yfc = S1f[:, (dc % 2) * cap:(dc % 2) * cap + cap]
                wve = [pe_mm]
                if ei == 0 and dc < 2:
                    wve.append(NOW("dve"))
                tev3 = op("vector", lambda e, o=yfc, i=by[:, cy:cy + cap]:
                          e.tensor_copy(out=o, in_=i), waits=wve)
                bank_rd[bsl] = tev3
                for i in range(ci):
                    bslot = 6 + i % 2
                    byt, cyt = bank(bslot)
                    w = [tev3, bank_rd[bslot]]
                    tty = op("tensor", lambda e, o=byt[:, cyt:cyt + 128],
                             i=yfc[:, i * 128:i * 128 + 128]:
                             e.transpose(o, i, IDENT[:, :]), waits=w, inc="pe")
                    wev4 = [tty]
                    if dc == 0:
                        wev4.append(ystore_done.get(i))
                    if i % 2 == 0:
                        tev4 = op("vector", lambda e, o=YALL(i)[:, dc * 128:dc * 128 + 128],
                                  i=byt[:, cyt:cyt + 128]: e.tensor_copy(out=o, in_=i),
                                  waits=wev4)
                    else:
                        tev4 = op("scalar", lambda e, o=YALL(i)[:, dc * 128:dc * 128 + 128],
                                  i=byt[:, cyt:cyt + 128]:
                                  e.activation(out=o, in_=i, func=AF.Copy),
                                  waits=wev4, inc="act")
                    bank_rd[bslot] = tev4
                    yrow_last[i] = tev4
            for i in range(ci):
                ystore_done[i] = dma("sync",
                                     y_dram[eb + i * 128:eb + (i + 1) * 128, :],
                                     YALL(i)[:, :], waits=[yrow_last[i]])
        PE_EXP = NOW("pe")
        Y_DMS = NOW("dms")
        DVE_EXP = NOW("dve")
        gbA = lambda q: XT[:, q * 2048:(q + 1) * 2048]
        gbB = lambda q: HBf[:, q * 2048:(q + 1) * 2048]
        cbv = lambda q: AR2f[:, q * 2048:(q + 1) * 2048]
        g_tok = {}
        cb_done = {0: None, 1: None}
        tr_done = {0: PE_EXP, 1: PE_EXP}

        def issue_gather(fi, extra=()):
            q = fi % 2
            w0 = list(extra) + [cb_done[q]]
            t0g = op("gpsimd", lambda e, c=2 * fi, q_=q: e.indirect_dma_start(
                gbA(q_), None, y_dram[:, :], IOA(DSTIB[:, c:c + 1], 0)),
                waits=w0, inc="dmg", amt=16)
            t1g = op("gpsimd", lambda e, c=2 * fi + 1, q_=q: e.indirect_dma_start(
                gbB(q_), None, y_dram[:, :], IOA(DSTIB[:, c:c + 1], 0)),
                inc="dmg", amt=16)
            g_tok[fi] = (t0g, t1g)

        issue_gather(0, extra=[Y_DMS, PE_EXP, DVE_EXP])
        issue_gather(1)
        ld2 = {}
        st_prev = {}
        for g in range(NG):
            t0c = g * TG
            acc_toks = {fc: None for fc in range(FC)}
            for fc in range(FC):
                w = [st_prev.get(fc)]
                if g == 0 and fc == 0:
                    w += [PE_EXP, DVE_EXP, Y_DMS]
                ld2[fc] = dma("sync", AR1[:, fc * TG:(fc + 1) * TG],
                              dst[fc, :, t0c:t0c + TG], waits=w)
            LD_ALL = NOW("dms")
            for tt in range(8):
                fi = g * 8 + tt
                q = fi % 2
                tA, tB = g_tok[fi]
                op("vector", lambda e, q_=q, c=2 * fi: e.tensor_scalar(
                    out=cbv(q_), in0=gbA(q_), scalar1=CRB[:, c:c + 1], scalar2=None,
                    op0=ALU.mult), waits=[tA, tr_done[q]] + ([LD_ALL] if tt == 0 else []))
                c1 = op("vector", lambda e, q_=q, c=2 * fi + 1: e.scalar_tensor_tensor(
                    out=cbv(q_), in0=gbB(q_), scalar=CRB[:, c:c + 1], in1=cbv(q_),
                    op0=ALU.mult, op1=ALU.add), waits=[tB])
                cb_done[q] = c1
                if fi + 2 < NG * 8:
                    issue_gather(fi + 2)
                for fc in range(FC):
                    bslot = 6 + fc % 2
                    bat, cat = bank(bslot)
                    tta = op("tensor", lambda e, o=bat[:, cat:cat + 128], q_=q, f_=fc:
                             e.transpose(o, cbv(q_)[:, f_ * 128:f_ * 128 + 128], IDENT[:, :]),
                             waits=[c1, bank_rd[bslot]], inc="pe")
                    tacc = op("vector", lambda e, o=AR1[:, fc * TG + tt * 128:
                                                       fc * TG + tt * 128 + 128],
                              i=bat[:, cat:cat + 128]:
                              e.scalar_tensor_tensor(out=o, in0=i, scalar=0.5, in1=o,
                                                     op0=ALU.mult, op1=ALU.add),
                              waits=[tta, ld2[fc], acc_toks[fc]])
                    bank_rd[bslot] = tacc
                    acc_toks[fc] = tacc
                tr_done[q] = ("pe", cnt["pe"])
            st_tok = NOW("dve")
            for fc in range(FC):
                st_prev[fc] = dma("sync", dst[fc, :, t0c:t0c + TG],
                                  AR1[:, fc * TG:(fc + 1) * TG], waits=[st_tok])
            add_entry([NOW("dms"), NOW("pe")])
        return

    emit_ffn_sparse(x0, x_state, CAP1, 0, echo_write=True)
    add_entry([NOW("dms")])

    asrc = x_state
    stok = {0: None, 1: None}
    cast_prev = {0: None, 1: None}
    for fc in range(FC):
        par = fc % 2
        xtp = XT[:, par * TPC:par * TPC + TPC]
        ldt = dma("sync", xtp, asrc[fc, :, 0:TPC], waits=[cast_prev[par]])
        xb = S1[:, 0:TPC] if par == 0 else SHH[:, 0:TPC]
        sq = S1[:, TPC:2 * TPC] if par == 0 else SHH[:, TPC:2 * TPC]
        tcast = op("vector", lambda e, o=xb, i=xtp: e.tensor_copy(out=o, in_=i),
                   waits=[ldt, stok[par]], inc="dve")
        cast_prev[par] = tcast
        tsq = op("scalar", lambda e, o=sq, i=xb: e.activation(out=o, in_=i, func=AF.Square),
                 waits=[tcast], inc="act")
        first, last = fc == 0, fc == FC - 1
        for seg in range(4):
            bsum, csum = bank(seg)
            bsq_, csq = bank(4 + seg)
            op("tensor", lambda e, o=bsum[0:1, csum:csum + 512], l=ONESb[:, 0:1],
               r=xb[:, seg * 512:seg * 512 + 512], st=first, sp=last:
               e.matmul(o, l, r, start=st, stop=sp), waits=[tcast, tsq])
            ptok = op("tensor", lambda e, o=bsq_[0:1, csq:csq + 512], l=ONESb[:, 1:2],
                      r=sq[:, seg * 512:seg * 512 + 512], st=first, sp=last:
                      e.matmul(o, l, r, start=st, stop=sp), inc="pe")
        stok[par] = ptok
    pe_st = NOW("pe")
    pe_bc = None
    for seg in range(4):
        pe_bc = ln_seg_chain(seg, 4 + seg, pe_st, pe_bc)
    xn_prev = None
    for fc in range(FC):
        xtp = XT[:, (fc % 2) * TPC:(fc % 2) * TPC + TPC]
        ld2 = dma("sync", xtp, asrc[fc, :, 0:TPC], waits=[xn_prev])
        for seg in range(4):
            br, cr = bank(seg)
            bn, cn = bank(4 + seg)
            xts = xtp[:, seg * 512:seg * 512 + 512]
            scrf = HBf[:, seg * 512:seg * 512 + 512]
            op("vector", lambda e, o=scrf, a=xts, b=br[:, cr:cr + 512]:
               e.tensor_tensor(out=o, in0=a, in1=b, op=ALU.mult), waits=[ld2, pe_bc])
            t2 = op("vector", lambda e,
                    o=AR1b[:, fc * TPC + seg * 512:fc * TPC + seg * 512 + 512],
                    a=scrf, b=bn[:, cn:cn + 512]:
                    e.tensor_tensor(out=o, in0=a, in1=b, op=ALU.add), inc="dve")
        xn_prev = t2
    dve_xn2 = NOW("dve")
    XN = lambda kc: AR1b[:, kc * TPC:(kc + 1) * TPC]

    QB = HB[:, 0:2048]
    KB = HB[:, 2048:4096]
    KT = HB[:, 4096:5120]
    VB = HB[:, 5120:7168]
    VTBASE = 7168
    VT = lambda tt: HB[:, VTBASE + tt * 136:VTBASE + tt * 136 + 136]
    tones_v = op("vector", lambda e: e.memset(
        HB[:, VTBASE + 128:VTBASE + 16 * 136:136], 1.0), waits=[dve_xn2], inc="dve")

    kv_toks = []
    qs_done = {}
    bk_rd = {i: None for i in range(8)}
    t_kvbA = t_kvbB = None
    for h in range(NH):
        base = 64 * (h % 2)
        qcol = (h // 2) * 2048
        pq, tq = load_panel(wqT[h, :, :])
        pk, tk = load_panel(wkT[h, :, :])
        pv, tv_ = load_panel(wvT[h, :, :])
        qb_toks = []
        for ts in range(4):
            bsl = ts % 2
            bq_, cq_ = bank(bsl)
            wv = [tq, dve_xn2, bk_rd[bsl]]
            for kc in range(FC):
                st, sp = kc == 0, kc == FC - 1
                op("tensor", lambda e, o=bq_[:, cq_:cq_ + 512],
                   l=PAN[pq][:, kc * 128:(kc + 1) * 128],
                   r=XN(kc)[:, ts * 512:ts * 512 + 512], st=st, sp=sp:
                   e.matmul(o, l, r, start=st, stop=sp),
                   waits=wv if st else (), inc="pe" if sp else None)
            pe_mm = ("pe", cnt["pe"])
            mark_panel(pq, pe_mm)
            tev = op("scalar", lambda e, o=QB[:, ts * 512:ts * 512 + 512],
                     i=bq_[:, cq_:cq_ + 512], b=BIAS[:, 0 * FC + h:0 * FC + h + 1]:
                     e.activation(out=o, in_=i, func=AF.Identity, bias=b),
                     waits=[pe_mm, CONST_DMS], inc="act")
            bk_rd[bsl] = tev
            qb_toks.append(tev)
        qs_t = []
        for ts in range(4):
            bsl = 4 + ts % 2
            bqs, cqs = bank(bsl)
            tmm = op("tensor", lambda e, o=bqs[0:64, cqs:cqs + 512], l=WMB[:, 0:64],
                     r=QB[:, ts * 512:ts * 512 + 512]:
                     e.matmul(o, l, r, start=True, stop=True),
                     waits=[qb_toks[ts], CONST_DMG, bk_rd[bsl]], inc="pe")
            scr1 = XT[0:64, (ts % 2) * 512:(ts % 2) * 512 + 512]
            scr2 = XT[0:64, 1024 + (ts % 2) * 512:1024 + (ts % 2) * 512 + 512]
            t1 = op("vector", lambda e, o=scr1, i=bqs[0:64, cqs:cqs + 512]:
                    e.tensor_scalar_min(o, i, 0.0),
                    waits=[tmm], inc="dve")
            t2 = op("scalar", lambda e, o=scr2, i=scr1:
                    e.activation(out=o, in_=i, func=AF.Exp), waits=[t1], inc="act")
            t3 = op("vector", lambda e,
                    o=AR2[base:base + 64, qcol + ts * 512:qcol + ts * 512 + 512],
                    i=bqs[0:64, cqs:cqs + 512], x2=scr2:
                    e.scalar_tensor_tensor(out=o, in0=i, scalar=0.0, in1=x2,
                                           op0=ALU.max, op1=ALU.add),
                    waits=[t2], inc="dve")
            bk_rd[bsl] = t3
            qs_t.append(t3)
        qs_done[h] = qs_t[-1]
        kb_toks = []
        for ts in range(4):
            bsl = ts % 2
            bk_, ck_ = bank(bsl)
            wv = [tk, dve_xn2, bk_rd[bsl]]
            for kc in range(FC):
                st, sp = kc == 0, kc == FC - 1
                op("tensor", lambda e, o=bk_[:, ck_:ck_ + 512],
                   l=PAN[pk][:, kc * 128:(kc + 1) * 128],
                   r=XN(kc)[:, ts * 512:ts * 512 + 512], st=st, sp=sp:
                   e.matmul(o, l, r, start=st, stop=sp),
                   waits=wv if st else (), inc="pe" if sp else None)
            pe_mm = ("pe", cnt["pe"])
            mark_panel(pk, pe_mm)
            tev = op("scalar", lambda e, o=KB[:, ts * 512:ts * 512 + 512],
                     i=bk_[:, ck_:ck_ + 512], b=BIAS[:, 1 * FC + h:1 * FC + h + 1]:
                     e.activation(out=o, in_=i, func=AF.Identity, bias=b),
                     waits=[pe_mm], inc="act")
            bk_rd[bsl] = tev
            kb_toks.append(tev)
        ks_t = []
        for grp in range(2):
            bsl = 6 + grp
            bks, cks = bank(bsl)
            wv = kb_toks + [CONST_DMG, bk_rd[bsl]]
            for j in range(8):
                tt = grp * 8 + j
                st, sp = j == 0, j == 7
                op("tensor", lambda e, o=bks[:, cks + j * 64:cks + j * 64 + 64],
                   l=KB[:, tt * 128:tt * 128 + 128], r=WMB[:, 64:128], st=st, sp=sp:
                   e.matmul(o, l, r, start=st, stop=sp),
                   waits=wv if st else (), inc="pe" if sp else None)
            pe_mm = ("pe", cnt["pe"])
            scr1 = XT[:, grp * 512:grp * 512 + 512]
            scr2 = XT[:, 1024 + grp * 512:1024 + grp * 512 + 512]
            t1 = op("vector", lambda e, o=scr1, i=bks[:, cks:cks + 512]:
                    e.tensor_scalar_min(o, i, 0.0), waits=[pe_mm], inc="dve")
            t2 = op("scalar", lambda e, o=scr2, i=scr1:
                    e.activation(out=o, in_=i, func=AF.Exp), waits=[t1], inc="act")
            t3 = op("vector", lambda e, o=KT[:, grp * 512:grp * 512 + 512],
                    i=bks[:, cks:cks + 512], x2=scr2:
                    e.scalar_tensor_tensor(out=o, in0=i, scalar=0.0, in1=x2,
                                           op0=ALU.max, op1=ALU.add), waits=[t2], inc="dve")
            bk_rd[bsl] = t3
            ks_t.append(t3)
        vb_toks = []
        for ts in range(4):
            bsl = ts % 2
            bv_, cv_ = bank(bsl)
            wv = [tv_, dve_xn2, bk_rd[bsl]]
            for kc in range(FC):
                st, sp = kc == 0, kc == FC - 1
                op("tensor", lambda e, o=bv_[:, cv_:cv_ + 512],
                   l=PAN[pv][:, kc * 128:(kc + 1) * 128],
                   r=XN(kc)[:, ts * 512:ts * 512 + 512], st=st, sp=sp:
                   e.matmul(o, l, r, start=st, stop=sp),
                   waits=wv if st else (), inc="pe" if sp else None)
            pe_mm = ("pe", cnt["pe"])
            mark_panel(pv, pe_mm)
            tev = op("scalar", lambda e, o=VB[:, ts * 512:ts * 512 + 512],
                     i=bv_[:, cv_:cv_ + 512], b=BIAS[:, 2 * FC + h:2 * FC + h + 1]:
                     e.activation(out=o, in_=i, func=AF.Identity, bias=b),
                     waits=[pe_mm], inc="act")
            bk_rd[bsl] = tev
            vb_toks.append(tev)
        vt_toks = []
        vt4 = HB[:, VTBASE:VTBASE + 16 * 136].rearrange("p (t c) -> p t c", c=136)
        for tq in range(4):
            bslot = 2 + tq % 2
            bxb, cxb = bankb(bslot)
            tq_last = None
            for kk in range(4):
                tt = tq * 4 + kk
                wv = [vb_toks[tq], tones_v, bk_rd[bslot]] if kk == 0 else ()
                tq_last = op("tensor", lambda e, o=bxb[:, cxb + kk * 128:cxb + kk * 128 + 128],
                             i=VB[:, tt * 128:tt * 128 + 128]:
                             e.transpose(o, i, IDENTB[:, :]), waits=wv,
                             inc="pe" if kk == 3 else None)
            tev = op("vector", lambda e, o=vt4[:, tq * 4:tq * 4 + 4, 0:128],
                     i=bxb[:, cxb:cxb + 512].rearrange("p (t c) -> p t c", c=128):
                     e.tensor_copy(out=o, in_=i), waits=[tq_last], inc="dve")
            bk_rd[bslot] = tev
            vt_toks.append(tev)
        bsl = 4 + (h % 2)
        bkv, ckv = bank(bsl)
        wv = ks_t + vt_toks + [bk_rd[bsl]]
        for tt in range(16):
            st, sp = tt == 0, tt == 15
            op("tensor", lambda e, o=bkv[0:64, ckv:ckv + 129],
               l=KT[:, tt * 64:tt * 64 + 64], r=VT(tt)[:, 0:129], st=st, sp=sp:
               e.matmul(o, l, r, start=st, stop=sp),
               waits=wv if st else (), inc="pe" if sp else None)
        pe_kv = ("pe", cnt["pe"])
        tkv = op("vector", lambda e,
                 o=KVP[base:base + 64, (h // 2) * 132:(h // 2) * 132 + 129],
                 i=bkv[0:64, ckv:ckv + 129]: e.tensor_copy(out=o, in_=i),
                 waits=[pe_kv], inc="dve")
        bk_rd[bsl] = tkv
        kv_toks.append(tkv)
        if h == 7:
            t_ccwA = dma("sync", cc_inA[:, :], KVP[:, 0:528], waits=kv_toks[:8])
            t_ccA = op("gpsimd", lambda e: e.collective_compute(
                "AllReduce", ALU.add, replica_groups=[[0, 1], [2, 3], [4, 5], [6, 7]],
                ins=[cc_inA.opt()], outs=[cc_outA.opt()]), waits=[t_ccwA], inc="ccs", amt=1)
            t_kvbA = dma("gpsimd", KVB[:, 0:528], cc_outA[:, :], waits=[t_ccA])
        if h == 15:
            t_ccwB = dma("sync", cc_inB[:, :], KVP[:, 528:1056], waits=kv_toks[8:])
            t_ccB = op("gpsimd", lambda e: e.collective_compute(
                "AllReduce", ALU.add, replica_groups=[[0, 1], [2, 3], [4, 5], [6, 7]],
                ins=[cc_inB.opt()], outs=[cc_outB.opt()]), waits=[t_ccwB], inc="cc2", amt=1)
            t_kvbB = dma("gpsimd", KVB[:, 528:1056], cc_outB[:, :], waits=[t_ccB])
    DVE_HEADS = NOW("dve")
    bk_rd = {i: DVE_HEADS for i in range(8)}
    ao_toks = []
    nrow_war = None
    for h in range(NH):
        base = 64 * (h % 2)
        qcol = (h // 2) * 2048
        kvcol = (h // 2) * 132
        t_kvb = t_kvbA if h < 8 else t_kvbB
        QS = AR2[base:base + 64, qcol:qcol + 2048]
        tmms = []
        for ts in range(4):
            bn_, cn_ = bank(4 + ts)
            tmms.append(op("tensor", lambda e, o=bn_[0:1, cn_:cn_ + 512],
                           l=KVB[base:base + 64, kvcol + 128:kvcol + 129],
                           r=QS[:, ts * 512:ts * 512 + 512]:
                           e.matmul(o, l, r, start=True, stop=True),
                           waits=[t_kvb, qs_done[h], bk_rd[4 + ts]], inc="pe"))
        op("vector", lambda e: e.tensor_scalar(
            out=NROWS[0:1, 0:1024], in0=PS[2][0:1, 0:1024], scalar1=EPS_AT,
            scalar2=None, op0=ALU.add), waits=[tmms[0], tmms[1], nrow_war], inc="dve")
        op("vector", lambda e: e.tensor_scalar(
            out=NROWS[0:1, 1024:2048], in0=PS[3][0:1, 0:1024], scalar1=EPS_AT,
            scalar2=None, op0=ALU.add), waits=[tmms[2], tmms[3]], inc="dve")
        trc = op("vector", lambda e: e.reciprocal(out=NROWS[0:1, 0:2048],
                                                  in_=NROWS[0:1, 0:2048]), inc="dve")
        qd_t = []
        for ts in range(4):
            bn_, cn_ = bank(4 + ts)
            tmb = op("tensor", lambda e, o=bn_[0:64, cn_:cn_ + 512],
                     r=NROWS[0:1, ts * 512:ts * 512 + 512]:
                     e.matmul(o, ONESf[0:1, 0:64], r, start=True, stop=True),
                     waits=[trc], inc="pe")
            if ts == 3:
                nrow_war = tmb
            tqd = op("vector", lambda e, o=HB[base:base + 64, ts * 512:ts * 512 + 512],
                     a=QS[:, ts * 512:ts * 512 + 512], b=bn_[0:64, cn_:cn_ + 512]:
                     e.tensor_tensor(out=o, in0=a, in1=b, op=ALU.mult),
                     waits=[tmb], inc="dve")
            bk_rd[4 + ts] = tqd
            qd_t.append(tqd)
        for ts in range(4):
            bsl = ts % 2
            bo_, co_ = bank(bsl)
            tmm = op("tensor", lambda e, o=bo_[:, co_:co_ + 512],
                     l=KVB[base:base + 64, kvcol:kvcol + 128],
                     r=HB[base:base + 64, ts * 512:ts * 512 + 512]:
                     e.matmul(o, l, r, start=True, stop=True),
                     waits=[qd_t[ts], t_kvb, bk_rd[bsl]], inc="pe")
            tev = op("scalar", lambda e,
                     o=AR1b[:, h * TPC + ts * 512:h * TPC + ts * 512 + 512],
                     i=bo_[:, co_:co_ + 512]: e.activation(out=o, in_=i, func=AF.Copy),
                     waits=[tmm], inc="act")
            bk_rd[bsl] = tev
            ao_toks.append(tev)
    AO = lambda hh: AR1b[:, hh * TPC:(hh + 1) * TPC]
    ldx_extra = [NOW("act"), NOW("dve")]
    st2 = {}
    p2dst = x_state
    for oc in range(FC):
        pw, tw = load_panel(woT[oc, :, :])
        xst = XT[:, (oc % 2) * TPC:(oc % 2) * TPC + TPC]
        wvx = list(ldx_extra) if oc < 2 else [st2[oc - 2]]
        ldx = dma("sync", xst, asrc[oc, :, 0:TPC], waits=wvx)
        for ts in range(4):
            bsl = 2 + ts % 2
            bw_, cw_ = bank(bsl)
            wv2 = [tw, bk_rd[bsl]] + (ao_toks if (oc == 0 and ts == 0) else [ao_toks[-1]])
            for hh in range(NH):
                st, sp = hh == 0, hh == NH - 1
                op("tensor", lambda e, o=bw_[:, cw_:cw_ + 512],
                   l=PAN[pw][:, hh * 128:(hh + 1) * 128],
                   r=AO(hh)[:, ts * 512:ts * 512 + 512], st=st, sp=sp:
                   e.matmul(o, l, r, start=st, stop=sp),
                   waits=wv2 if st else (), inc="pe" if sp else None)
            pe_mm = ("pe", cnt["pe"])
            mark_panel(pw, pe_mm)
            tev = op("vector", lambda e, o=xst[:, ts * 512:ts * 512 + 512],
                     i=bw_[:, cw_:cw_ + 512], b=BIAS[:, 3 * FC + oc:3 * FC + oc + 1]:
                     e.scalar_tensor_tensor(out=o, in0=i, scalar=b, in1=o,
                                            op0=ALU.add, op1=ALU.add),
                     waits=[pe_mm, ldx], inc="dve")
            bk_rd[bsl] = tev
        st2[oc] = dma("sync", p2dst[oc, :, 0:TPC], xst, waits=[bk_rd[2], bk_rd[3]])
    add_entry([NOW("dms")])

    add_entry([NOW("dml")])
    emit_ffn_sparse(x_state, y_out, CAP3, 1, echo_write=False)

    op("sync", lambda e: e.nop(), waits=[NOW("dms"), NOW("dmg"), NOW("dml")])

    with nc.Block() as block:
        def mk(engname):
            own = SELF_SER.get(engname)

            def body(eng):
                lw = {}
                done = 0
                for fn, waits, inc, amt in prog[engname]:
                    if own is not None and done > 0:
                        eng.wait_ge(sems[own], done)
                    for (s, v) in waits:
                        if lw.get(s, 0) < v:
                            eng.wait_ge(sems[s], v)
                            lw[s] = v
                    instr = fn(eng)
                    if inc is not None:
                        instr.then_inc(sems[inc], amt)
                        if inc == own:
                            done += amt
            return body
        block.sync(mk("sync"))
        block.scalar(mk("scalar"))
        block.vector(mk("vector"))
        block.tensor(mk("tensor"))
        block.gpsimd(mk("gpsimd"))
    ctx.close()
    return nc


def _prep_inputs(kw):
    f32 = np.float32
    x = np.asarray(kw["L0_attn_hidden_states"], f32).reshape(B * S, D)
    w1 = np.concatenate([np.asarray(kw["L0_ffn_sw1"], f32),
                         np.asarray(kw["L0_ffn_w1"], f32)], axis=0)
    w2 = np.concatenate([np.asarray(kw["L0_ffn_sw2"], f32),
                         np.asarray(kw["L0_ffn_w2"], f32)], axis=0)
    w1R = np.ascontiguousarray(
        w1.reshape(NEX, FC, 128, HC, 128).transpose(0, 3, 2, 1, 4)).reshape(NEX, HC, 128, FC * 128)
    w2R = np.ascontiguousarray(
        w2.reshape(NEX, HC, 128, FC, 128).transpose(0, 3, 2, 1, 4)).reshape(NEX, FC, 128, HC * 128)

    def wt(a):
        at = np.ascontiguousarray(np.asarray(a, f32).T)
        return np.ascontiguousarray(
            at.reshape(FC, 128, FC, 128).transpose(2, 1, 0, 3)).reshape(FC, 128, FC * 128)

    wq = wt(kw["L0_attn_wq"]); wk = wt(kw["L0_attn_wk"])
    wv = wt(kw["L0_attn_wv"]); wo = wt(kw["L0_attn_wo"])
    gate = np.ascontiguousarray(
        np.asarray(kw["L0_ffn_gate_w"], f32).T.reshape(FC, 128, NE).transpose(1, 0, 2)
    ).reshape(128, FC * NE)
    biasC = np.ascontiguousarray(
        np.stack([np.asarray(kw["L0_attn_bq"], f32), np.asarray(kw["L0_attn_bk"], f32),
                  np.asarray(kw["L0_attn_bv"], f32), np.asarray(kw["L0_attn_bo"], f32)]
                 ).reshape(4, FC, 128).transpose(2, 0, 1)).reshape(128, 4 * FC)
    wmapT = np.ascontiguousarray(
        np.concatenate([np.asarray(kw["L0_attn_wmap_q"], f32).T,
                        np.asarray(kw["L0_attn_wmap_k"], f32).T], axis=1))
    ident = np.eye(128, dtype=f32)

    expbase = np.stack([np.arange(16, dtype=f32) * CAP1,
                        np.arange(16, dtype=f32) * CAP3], axis=1)
    shared = dict(w1R=w1R, w2R=w2R, wqT=wq, wkT=wk, wvT=wv, woT=wo, gateT=gate,
                  biasC=biasC, wmapT=wmapT, ident_in=ident, expbase_in=expbase)
    in_maps = []
    for c in range(NCORES):
        xc = np.ascontiguousarray(x[c * TPC:(c + 1) * TPC, :].T).reshape(FC, 128, TPC)
        m = {"x0": xc}
        m.update(shared)
        in_maps.append(m)
    return in_maps


def kernel(**kw):
    in_maps = _prep_inputs(kw)
    nc = build_program()
    res = run_bass_kernel_spmd(nc, in_maps, core_ids=list(range(NCORES)))
    outs = []
    for c in range(NCORES):
        yc = np.asarray(res.results[c]["y"]).reshape(D, TPC)
        outs.append(np.ascontiguousarray(yc.T))
    return np.concatenate(outs, axis=0).reshape(B, S, D).astype(np.float32)
